# version 20
# baseline (speedup 1.0000x reference)
import sys
sys.path.insert(0, "/opt/trn_rl_repo")
import numpy as np
import ml_dtypes
import concourse.bass as bass
import concourse.mybir as mybir
from concourse.bass_utils import run_bass_kernel_spmd
from concourse.ap import AP
from contextlib import ExitStack
import concourse.bass_utils as _bu

if not getattr(_bu, "_ldwopt_patched", False):
    _orig_run_command = _bu.run_command

    def _patched_run_command(cmd, *a, **kw):
        if isinstance(cmd, list):
            cmd = [c.replace("--enable-ldw-opt=false", "--enable-ldw-opt=false")
                   if isinstance(c, str) else c for c in cmd]
        return _orig_run_command(cmd, *a, **kw)
    _bu.run_command = _patched_run_command
    _bu._ldwopt_patched = True

F32 = mybir.dt.float32
BF16 = mybir.dt.bfloat16
I32 = mybir.dt.int32
AF = mybir.ActivationFunctionType
ALU = mybir.AluOpType
AXX = mybir.AxisListType.X

S = 2048
D = 1024
NH = 16
NPAIR = 8
NSEQ = 2
DC = 8
FH = 3072
FHC = 24
FSUP = 8
NTT = 4
WTP = 2176
EPW = 4097
EPS = 1e-5
DQ = 1.0 / 127.0

O_H1 = 0
O_COMB = 16384
O_QKV = 32768
O_SIG = 38912
O_AEXP = 40960
O_FH2 = 41984
O_TP = 43008
O_BSH = 45184
O_FY2 = 46080
O_AT = 47232
O_GS = 73985
O_VTM = 55424
O_XB = 57472
O_XB2 = 59520
O_E2 = 61568
O_WP = 65665
O_WO = 68737
O_QZ = O_WO
O_TP2 = 69761
O_BSH2 = 71937
O_AEXP2 = 74497
ANE = 76545

X_A = 0
X_B = 1024
X_ROW = 2048
X_NT = 5120
XNE = 9216

ENGS = ("tensor", "vector", "scalar", "sync", "gpsimd")


class _B:
    def __init__(self, sems):
        self.sems = sems
        self.owner = {"pe": "tensor", "vx": "vector", "sc": "scalar",
                      "sq": "sync", "gq": "gpsimd"}
        self.val = {k: 0 for k in sems}
        self.ops = {e: [] for e in ENGS}
        self.lastw = {e: {} for e in ENGS}

    def do(self, eng, fn, waits=(), inc=None, n=1):
        ws = []
        for (s, v) in waits:
            if v is None or v <= 0:
                continue
            if self.lastw[eng].get(s, 0) >= v:
                continue
            self.lastw[eng][s] = v
            ws.append((s, v))
        if inc is not None:
            self.val[inc] += n
        self.ops[eng].append((ws, fn, inc, n))
        return self.val[inc] if inc is not None else None

    def check(self):
        idx = {e: 0 for e in ENGS}
        cur = {k: 0 for k in self.val}
        progress = True
        while progress:
            progress = False
            for e in ENGS:
                while idx[e] < len(self.ops[e]):
                    ws, fn, inc, n = self.ops[e][idx[e]]
                    if any(cur[s] < v for (s, v) in ws):
                        break
                    if inc is not None:
                        cur[inc] += n
                    idx[e] += 1
                    progress = True
        stuck = {e: (idx[e], len(self.ops[e])) for e in ENGS
                 if idx[e] < len(self.ops[e])}
        if stuck:
            msg = []
            for e, (i, t) in stuck.items():
                msg.append(f"{e}@{i}/{t} waits {self.ops[e][i][0]}")
            raise RuntimeError("static deadlock: " + "; ".join(msg))


def build_program(dbg=frozenset(), npair=NPAIR, nseq=NSEQ, stage="full", skip=frozenset()):
    nc = bass.Bass("TRN2", target_bir_lowering=False)
    x_d = nc.dram_tensor("x", [NSEQ, D, S], F32, kind="ExternalInput")
    w_d = {nm: nc.dram_tensor(nm, [D, D], F32, kind="ExternalInput")
           for nm in ("wqT", "wkT", "wvT", "woT")}
    b_d = {nm: nc.dram_tensor(nm, [128, 8], F32, kind="ExternalInput")
           for nm in ("bq", "bk", "bv", "bo")}
    e2_d = nc.dram_tensor("e2d", [128, EPW], F32, kind="ExternalInput")
    ma_d = nc.dram_tensor("maugd", [128, NH, 128], F32, kind="ExternalInput")
    wg_d = nc.dram_tensor("wgT", [D, FH], I32, kind="ExternalInput")
    wu_d = nc.dram_tensor("wuT", [D, FH], I32, kind="ExternalInput")
    wd_d = nc.dram_tensor("wdT", [FH, D], I32, kind="ExternalInput")
    ga_d = nc.dram_tensor("galpha", [1, 1], F32, kind="ExternalInput")
    zb_d = nc.dram_tensor("zbd", [128, 16], BF16, kind="ExternalInput")
    id_d = nc.dram_tensor("idd", [128, 128], BF16, kind="ExternalInput")
    sel_d = nc.dram_tensor("seld", [2, 128], BF16, kind="ExternalInput")
    xs_d = nc.dram_tensor("xscr", [NSEQ, D, S], F32, kind="Internal")
    y_d = nc.dram_tensor("yout", [NSEQ, D, S], F32, kind="ExternalOutput")

    taps = {}
    tapspec = {"h1": ([128, DC, S], BF16), "qkv": ([128, 3, S], BF16),
               "tp": ([128, WTP], BF16), "bsh": ([128, S], BF16),
               "aexp": ([128, S], BF16), "cols": ([128, 32], F32),
               "comb": ([128, DC, S], BF16), "y1": ([128, DC, S], BF16),
               "sig": ([128, S], BF16), "at": ([128, 16, 512], BF16),
               "vtm": ([128, 16, 128], BF16), "h2": ([128, DC, 512], BF16),
               "gdbg": ([128, 2048], F32), "gdb2": ([128, 1024], F32),
               "qkv3": ([128, 3, S], BF16), "sig3": ([128, S], BF16),
               "h3": ([128, FHC, 512], BF16), "y2": ([128, DC, 512], BF16)}
    for t in dbg:
        sh, dt_ = tapspec[t]
        taps[t] = nc.dram_tensor("tap_" + t, sh, dt_, kind="ExternalOutput")

    with ExitStack() as st:
        AB = st.enter_context(nc.sbuf_tensor("AB", [128, ANE], BF16))
        XF = st.enter_context(nc.sbuf_tensor("XF", [128, XNE], F32))
        SF = st.enter_context(nc.sbuf_tensor("SF", [128, 512], F32))
        SB = st.enter_context(nc.sbuf_tensor("SB", [128, 2048], BF16))
        UB = st.enter_context(nc.sbuf_tensor("UB", [128, 16], F32))
        CF = st.enter_context(nc.sbuf_tensor("CF", [128, 32], F32))
        CB = st.enter_context(nc.sbuf_tensor("CB", [128, 16], BF16))
        BIA = st.enter_context(nc.sbuf_tensor("BIA", [128, 32], F32))
        MAU = st.enter_context(nc.sbuf_tensor("MAU", [128, NH, 128], BF16))
        IDT = st.enter_context(nc.sbuf_tensor("IDT", [128, 128], BF16))
        ONB = st.enter_context(nc.sbuf_tensor("ONB", [128, 4], BF16))
        ONR = st.enter_context(nc.sbuf_tensor("ONR", [128, 128], BF16))
        ON32 = st.enter_context(nc.sbuf_tensor("ON32", [1, 128], F32))
        SEL = st.enter_context(nc.sbuf_tensor("SEL", [2, 128], BF16))
        ZB = st.enter_context(nc.sbuf_tensor("ZB", [128, 16], BF16))
        QM = st.enter_context(nc.sbuf_tensor("QM", [128, 2], F32))
        GA = st.enter_context(nc.sbuf_tensor("GA", [1, 1], F32))
        CONSTS = st.enter_context(nc.sbuf_tensor("CONSTS", [128, 10], F32))
        psT = st.enter_context(nc.psum_tensor("psT", [128, 2048], F32))
        psS = st.enter_context(nc.psum_tensor("psS", [128, 1024], F32))
        psR = st.enter_context(nc.psum_tensor("psR", [128, 1024], F32))
        psRb = psR[:].bitcast(BF16)
        sems = {k: st.enter_context(nc.semaphore(k))
                for k in ("pe", "vx", "sc", "sq", "gq")}
        P = _B(sems)

        def ab(p0, np_, off, dims):
            return AP(AB, p0 * ANE + off, [[ANE, np_]] + [list(dd) for dd in dims])

        def xf(p0, np_, off, dims):
            return AP(XF, p0 * XNE + off, [[XNE, np_]] + [list(dd) for dd in dims])

        def combv(p0, np_, dc, c0, n):
            return ab(p0, np_, O_COMB + dc * 2048 + c0, [[1, n]])

        (C_Z0, C_Z1, C_Z, C_C1, C_G, C_G1M, C_NKE, C_E2M, C_M2, C_TMP,
         C_NK2, C_LNG, C_LN1MG, C_C1B) = range(14)
        CB_NK2 = 0

        def c1c(ti):
            return C_C1 if ti % 2 == 0 else C_C1B

        free = {k: {} for k in
                ("psT", "psS", "psR", "TP", "BSH", "AEXP", "AT", "VTM", "QKV",
                 "SIG", "H1", "COMB", "WP", "WO", "XA", "XBUF", "XB", "XB2",
                 "SBR", "UBR", "CBR", "FFNW", "GS", "H2", "Y2", "OUT", "XROW",
                 "XNT")}

        def fmark(rg, sem, v):
            if v is not None and free[rg].get(sem, 0) < v:
                free[rg][sem] = v

        def fclr(rg):
            free[rg] = {}

        def fw(*rgs):
            out = {}
            for rg in rgs:
                for s, v in free[rg].items():
                    if out.get(s, 0) < v:
                        out[s] = v
            return list(out.items())

        LN8 = float(np.log(1.0 / 8.0))

        vv = None
        for f in (lambda v: v.memset(ab(0, 128, O_VTM, [[1, 2048]]), 0.0),
                  lambda v: v.memset(CF[:], 0.0),
                  lambda v: v.memset(CB[:], 0.0),
                  lambda v: v.memset(UB[:], 0.0),
                  lambda v: v.memset(ONB[:], 0.0),
                  lambda v: v.memset(ONB[:, 0:1], 1.0),
                  lambda v: v.memset(ONB[0:64, 1:2], 1.0),
                  lambda v: v.memset(ONB[64:128, 2:3], 1.0),
                  lambda v: v.memset(ONR[:], 1.0),
                  lambda v: v.memset(ON32[:], 1.0),
                  lambda v: v.memset(CF[0:2, C_E2M:C_E2M + 1], 0.0),
                  lambda v: v.memset(QM[:], 0.0),
                  lambda v: v.memset(QM[0:64, 0:1], 1.0),
                  lambda v: v.memset(QM[64:128, 1:2], 1.0)):
            vv = P.do("vector", f, waits=([("vx", vv)] if vv else []), inc="vx")
        _cvals = [0.0, 1.0, -1.0, 0.125, 0.5, LN8, EPS, DQ, 1e-6, -0.5]
        for j, cv in enumerate(_cvals):
            vv = P.do("vector", lambda v, j=j, cv=cv: v.memset(
                CONSTS[:, j:j + 1], cv), inc="vx")
            nc.const_aps.register(F32, cv, CONSTS[:, j:j + 1])
        v_consts = vv

        qv = P.do("sync", lambda s: s.dma_start(IDT[:], id_d[:]), inc="sq", n=16)
        qv = P.do("sync", lambda s: s.dma_start(SEL[:], sel_d[:]), inc="sq", n=16)
        for i, nm in enumerate(("bq", "bk", "bv", "bo")):
            qv = P.do("sync", lambda s, nm=nm, i=i: s.dma_start(
                BIA[:, i * 8:(i + 1) * 8], b_d[nm][:]), inc="sq", n=16)
        qv = P.do("sync", lambda s: s.dma_start(GA[:], ga_d[:]), inc="sq", n=16)
        qv = P.do("sync", lambda s: s.dma_start(ZB[:], zb_d[:]), inc="sq", n=16)
        sq_pro = qv
        gv = P.do("gpsimd", lambda g: g.dma_start(
            ab(0, 128, O_E2, [[1, EPW]]), e2_d[:]), inc="gq", n=16)
        gv = P.do("gpsimd", lambda g: g.dma_start(MAU[:], ma_d[:]), inc="gq", n=16)
        gq_pro = gv

        sv = P.do("scalar", lambda s: s.activation(GA[:], GA[:], AF.Sigmoid),
                  waits=[("sq", sq_pro)], inc="sc")
        pv = P.do("tensor", lambda t: t.matmul(psS[:, 0:1], ON32[:], GA[:],
                                               start=True, stop=True),
                  waits=[("sc", sv), ("vx", v_consts)], inc="pe")
        vv = P.do("vector", lambda v: v.tensor_copy(CF[:, C_G:C_G + 1], psS[:, 0:1]),
                  waits=[("pe", pv)], inc="vx")
        vv = P.do("vector", lambda v: v.tensor_scalar(
            CF[:, C_G1M:C_G1M + 1], CF[:, C_G:C_G + 1], -1.0, 1.0, ALU.mult, ALU.add),
            waits=[("vx", vv)], inc="vx")
        fmark("psS", "vx", vv)
        sv = P.do("scalar", lambda s: s.activation(
            CF[:, C_LNG:C_LNG + 1], CF[:, C_G:C_G + 1], AF.Ln),
            waits=[("vx", vv)], inc="sc")
        sv = P.do("scalar", lambda s: s.activation(
            CF[:, C_LN1MG:C_LN1MG + 1], CF[:, C_G1M:C_G1M + 1], AF.Ln),
            waits=[("sc", sv)], inc="sc")

        for pc in range((EPW + 511) // 512):
            c0 = pc * 512
            n = min(512, EPW - c0)
            sv = P.do("scalar", lambda s, c0=c0, n=n: s.activation(
                ab(0, 128, O_XB, [[1, n]]), ab(0, 128, O_E2 + c0, [[1, n]]),
                AF.Square), waits=[("gq", gq_pro)] + fw("XB"), inc="sc")
            pv = P.do("tensor", lambda t, n=n: t.matmul(
                psS[0:2, 0:n], ONB[:, 1:3], ab(0, 128, O_XB, [[1, n]]),
                start=True, stop=True),
                waits=[("sc", sv)] + fw("psS"), inc="pe")
            fmark("XB", "pe", pv)
            vv = P.do("vector", lambda v, n=n: v.reduce_max(
                CF[0:2, C_TMP:C_TMP + 1], psS[0:2, 0:n], AXX),
                waits=[("pe", pv), ("vx", v_consts)], inc="vx")
            vv = P.do("vector", lambda v: v.tensor_max(
                CF[0:2, C_E2M:C_E2M + 1], CF[0:2, C_E2M:C_E2M + 1],
                CF[0:2, C_TMP:C_TMP + 1]), waits=[("vx", vv)], inc="vx")
            fmark("psS", "vx", vv)
        sv = P.do("scalar", lambda s: s.activation(
            CF[0:2, C_TMP:C_TMP + 1], CF[0:2, C_E2M:C_E2M + 1], AF.Ln),
            waits=[("vx", vv)], inc="sc")
        sc_ne = P.do("scalar", lambda s: s.activation(
            CF[0:2, C_E2M:C_E2M + 1], CF[0:2, C_TMP:C_TMP + 1], AF.Exp, scale=0.5),
            waits=[("sc", sv)], inc="sc")

        sq_out_total = [0]

        def phase_mark():
            for rg in free:
                for s in ("pe", "vx", "sc", "sq", "gq"):
                    fmark(rg, s, P.val[s])

        def emit_ln_stats(src_fn, wsrc_fn, bf16src, psum_sx, psum_sx2, nchunks=DC,
                          width=1024, src_rg=None):
            lastpe = None
            for c in range(nchunks):
                srcap = src_fn(c)
                wsrc = wsrc_fn(c)
                if bf16src:
                    xbap = srcap
                    vxv = None
                else:
                    vxv = P.do("vector", lambda v, srcap=srcap, width=width:
                               v.tensor_copy(
                        ab(0, 128, O_XB, [[1, width]]), srcap),
                        waits=wsrc + fw("XB"), inc="vx")
                    xbap = ab(0, 128, O_XB, [[1, width]])
                scv = P.do("scalar", lambda s, srcap=srcap, width=width:
                           s.activation(
                    ab(0, 128, O_XB2, [[1, width]]), srcap, AF.Square),
                    waits=wsrc + fw("XB2"), inc="sc")
                ng = width // 512
                for g2 in range(ng):
                    wssx = ([("vx", vxv)] if vxv else wsrc) + \
                        (fw("psT", "psS", "psR") if c == 0 and g2 == 0 else [])
                    pv = P.do("tensor", lambda t, g2=g2, c=c, xbap=xbap: t.matmul(
                        psum_sx(g2), ONB[:, 0:1],
                        AP(xbap.tensor, xbap.offset + g2 * 512,
                           [list(d) for d in xbap.ap[:-1]] + [[1, 512]]),
                        start=(c == 0), stop=(c == nchunks - 1)),
                        waits=wssx, inc="pe")
                    pv = P.do("tensor", lambda t, g2=g2, c=c: t.matmul(
                        psum_sx2(g2), ONB[:, 0:1],
                        ab(0, 128, O_XB2 + g2 * 512, [[1, 512]]),
                        start=(c == 0), stop=(c == nchunks - 1)),
                        waits=[("sc", scv)], inc="pe")
                lastpe = pv
                if src_rg is not None:
                    rg = src_rg(c)
                    if vxv is not None:
                        fmark(rg, "vx", vxv)
                    fmark(rg, "sc", scv)
                    if bf16src:
                        fmark(rg, "pe", pv)
                if not bf16src:
                    fmark("XB", "pe", lastpe)
                fmark("XB2", "pe", lastpe)
            return lastpe

        def emit_mrstd(pe_stats, sx_ap, sx2_ap, width, nd=D):
            m_ap = xf(0, 1, X_ROW, [[1, width]])
            v_ap = xf(0, 1, X_ROW + 1024, [[1, width]])
            t_ap = xf(0, 1, X_ROW + 2048, [[1, width]])
            wv = fw("XROW")
            vv = P.do("vector", lambda v: v.tensor_scalar(
                m_ap, sx_ap, 1.0 / nd, None, ALU.mult),
                waits=[("pe", pe_stats)] + wv, inc="vx")
            vv = P.do("vector", lambda v: v.tensor_scalar(
                v_ap, sx2_ap, 1.0 / nd, None, ALU.mult),
                waits=[("vx", vv)], inc="vx")
            vv = P.do("vector", lambda v: v.tensor_mul(t_ap, m_ap, m_ap),
                      waits=[("vx", vv)], inc="vx")
            vv = P.do("vector", lambda v: v.tensor_sub(v_ap, v_ap, t_ap),
                      waits=[("vx", vv)], inc="vx")
            sv = P.do("scalar", lambda s: s.activation(t_ap, v_ap, AF.Ln,
                                                       bias=EPS),
                      waits=[("vx", vv)], inc="sc")
            sv = P.do("scalar", lambda s: s.activation(v_ap, t_ap, AF.Exp,
                                                       scale=-0.5),
                      waits=[("sc", sv)], inc="sc")
            fmark("XROW", "sc", sv)
            wsb = fw("SBR")
            vv1 = P.do("vector", lambda v: v.tensor_copy(SB[0:1, 0:width], m_ap),
                       waits=wsb + [("vx", vv), ("sc", sv)], inc="vx")
            vv2 = P.do("vector", lambda v: v.tensor_copy(
                SB[0:1, 1024:1024 + width], v_ap), waits=[("vx", vv1)], inc="vx")
            return vv2

        def emit_bcast(vx_rows, g2, mcol0, psum_m, psum_r):
            pv = P.do("tensor", lambda t: t.matmul(
                psum_m, ONR[0:1, :], SB[0:1, mcol0 + g2 * 512:mcol0 + g2 * 512 + 512],
                start=True, stop=True), waits=[("vx", vx_rows)] + fw("psT"),
                inc="pe")
            pv = P.do("tensor", lambda t: t.matmul(
                psum_r, ONR[0:1, :],
                SB[0:1, 1024 + mcol0 + g2 * 512:1024 + mcol0 + g2 * 512 + 512],
                start=True, stop=True), inc="pe")
            return pv

        for s_i in range(nseq):
            for half in range(2):
                hc0 = half * 1024
                xsrc = {}

                def ln1src(c, hc0=hc0, xsrc=xsrc, s_i=s_i):
                    buf = X_A if c % 2 == 0 else X_B
                    rg = "XA" if c % 2 == 0 else "XBUF"
                    sqv = P.do("sync", lambda s, c=c, buf=buf, s_i=s_i, hc0=hc0: s.dma_start(
                        xf(0, 128, buf, [[1, 1024]]),
                        x_d[s_i, c * 128:(c + 1) * 128, hc0:hc0 + 1024]),
                        waits=fw(rg), inc="sq", n=16)
                    xsrc[c] = sqv
                    return xf(0, 128, buf, [[1, 1024]])

                pe_st = emit_ln_stats(
                    ln1src, lambda c: [("sq", xsrc[c])], False,
                    lambda g2: psT[0:1, g2 * 512:(g2 + 1) * 512],
                    lambda g2: psT[0:1, 1024 + g2 * 512:1024 + (g2 + 1) * 512],
                    src_rg=lambda c: "XA" if c % 2 == 0 else "XBUF")
                vxr = emit_mrstd(pe_st, psT[0:1, 0:1024], psT[0:1, 1024:2048], 1024)
                fmark("psT", "vx", vxr)
                fmark("XROW", "vx", vxr)
                pvb = []
                for g2 in range(2):
                    pv = emit_bcast(vxr, g2, 0,
                                    psT[:, g2 * 1024:g2 * 1024 + 512],
                                    psT[:, g2 * 1024 + 512:g2 * 1024 + 1024])
                    pvb.append(pv)
                fmark("SBR", "pe", pvb[-1])
                lastvx = None
                for c in range(DC):
                    buf = X_A if c % 2 == 0 else X_B
                    rg = "XA" if c % 2 == 0 else "XBUF"
                    sqv = P.do("sync", lambda s, c=c, buf=buf, s_i=s_i, hc0=hc0: s.dma_start(
                        xf(0, 128, buf, [[1, 1024]]),
                        x_d[s_i, c * 128:(c + 1) * 128, hc0:hc0 + 1024]),
                        waits=fw(rg), inc="sq", n=16)
                    for g2 in range(2):
                        vv = P.do("vector", lambda v, buf=buf, g2=g2: v.tensor_sub(
                            SF[:, 0:512], xf(0, 128, buf + g2 * 512, [[1, 512]]),
                            psT[:, g2 * 1024:g2 * 1024 + 512]),
                            waits=[("sq", sqv), ("pe", pvb[g2])] + fw("H1"),
                            inc="vx")
                        vv = P.do("vector", lambda v, c=c, g2=g2, hc0=hc0: v.tensor_mul(
                            ab(0, 128, O_H1 + c * 2048 + hc0 + g2 * 512, [[1, 512]]),
                            SF[:, 0:512],
                            psT[:, g2 * 1024 + 512:g2 * 1024 + 1024]),
                            waits=[("vx", vv)], inc="vx")
                    fmark(rg, "vx", vv)
                    lastvx = vv
                fmark("psT", "vx", lastvx)
                fclr("H1")
                fmark("H1", "vx", lastvx)
            vx_h1 = lastvx
            if "h1" in dbg and s_i == 0:
                qv = P.do("sync", lambda s: s.dma_start(
                    taps["h1"][:], ab(0, 128, O_H1, [[2048, DC], [1, S]])),
                    waits=[("vx", vx_h1)], inc="sq", n=16)

            if stage == "ln1":
                break
            for pr in range(npair):
                gws = []
                for wi, wnm in enumerate(("wqT", "wkT", "wvT")):
                    gv = P.do("gpsimd", lambda g, wi=wi, wnm=wnm, pr=pr: g.dma_start(
                        ab(0, 128, O_WP + wi * 1024, [[128, 8], [1, 128]]),
                        w_d[wnm][:, pr * 128:(pr + 1) * 128].rearrange(
                            "(c p) o -> p c o", p=128)),
                        waits=fw("WP"), inc="gq", n=16)
                    gws.append(gv)
                qkv_ev = {}
                for wi in range(3):
                    lastmm = None
                    for cg in range(4):
                        for dcc in range(DC):
                            wts = []
                            if dcc == 0:
                                wts = [("gq", gws[wi]), ("vx", vx_h1),
                                       ("vx", qkv_ev.get((wi - 1, cg)))]
                                if wi == 0 and cg == 0:
                                    wts += fw("psT")
                            lastmm = P.do("tensor", lambda t, cg=cg, dcc=dcc, wi=wi:
                                          t.matmul(
                                psT[:, cg * 512:(cg + 1) * 512],
                                ab(0, 128, O_WP + wi * 1024 + dcc * 128, [[1, 128]]),
                                ab(0, 128, O_H1 + dcc * 2048 + cg * 512, [[1, 512]]),
                                start=(dcc == 0), stop=(dcc == DC - 1)),
                                waits=wts, inc="pe")
                        vv = P.do("vector", lambda v, wi=wi, pr=pr, cg=cg:
                                  v.tensor_scalar(
                            ab(0, 128, O_QKV + wi * 2048 + cg * 512, [[1, 512]]),
                            psT[:, cg * 512:(cg + 1) * 512],
                            BIA[:, wi * 8 + pr:wi * 8 + pr + 1], None,
                            ALU.add), waits=[("pe", lastmm)] +
                            (fw("QKV") if wi == 0 and cg == 0 else []), inc="vx")
                        qkv_ev[(wi, cg)] = vv
                fmark("psT", "vx", vv)
                vx_qkv = vv
                vx_q = qkv_ev[(0, 3)]
                vx_k = qkv_ev[(1, 3)]
                fmark("WP", "pe", lastmm)
                if "qkv3" in dbg and s_i == 0 and pr == 3:
                    qv = P.do("sync", lambda s: s.dma_start(
                        taps["qkv3"][:], ab(0, 128, O_QKV, [[2048, 3], [1, S]])),
                        waits=[("vx", vx_qkv)], inc="sq", n=16)
                if "qkv" in dbg and s_i == 0 and pr == 0:
                    qv = P.do("sync", lambda s: s.dma_start(
                        taps["qkv"][:], ab(0, 128, O_QKV, [[2048, 3], [1, S]])),
                        waits=[("vx", vx_qkv)], inc="sq", n=16)

                sc_qsq = P.do("scalar", lambda s: s.activation(
                    ab(0, 128, O_XB, [[1, 2048]]),
                    ab(0, 128, O_QKV, [[1, 2048]]), AF.Square),
                    waits=[("vx", vx_q)] + fw("XB"), inc="sc")
                vv = P.do("vector", lambda v: v.tensor_scalar(
                    ab(0, 128, O_XB2, [[1, 2048]]),
                    ab(0, 128, O_QKV, [[1, 2048]]), 0.0, None, ALU.min),
                    waits=[("vx", vx_q)] + fw("XB2"), inc="vx")
                sv = P.do("scalar", lambda s: s.activation(
                    ab(0, 128, O_SIG, [[1, 2048]]),
                    ab(0, 128, O_XB2, [[1, 2048]]), AF.Exp),
                    waits=[("vx", vv)] + fw("SIG"), inc="sc")
                vx_sig = P.do("vector", lambda v: v.scalar_tensor_tensor(
                    ab(0, 128, O_SIG, [[1, 2048]]),
                    ab(0, 128, O_QKV, [[1, 2048]]), 0.0,
                    ab(0, 128, O_SIG, [[1, 2048]]), ALU.max, ALU.add),
                    waits=[("sc", sv)], inc="vx")
                fmark("XB2", "sc", sv)
                fmark("XB2", "vx", vx_sig)
                if "sig3" in dbg and s_i == 0 and pr == 3:
                    qv = P.do("sync", lambda s: s.dma_start(
                        taps["sig3"][:], ab(0, 128, O_SIG, [[1, S]])),
                        waits=[("vx", vx_sig)], inc="sq", n=16)
                if "sig" in dbg and s_i == 0 and pr == 0:
                    qv = P.do("sync", lambda s: s.dma_start(
                        taps["sig"][:], ab(0, 128, O_SIG, [[1, S]])),
                        waits=[("vx", vx_sig)], inc="sq", n=16)
                sc_ksq = P.do("scalar", lambda s: s.activation(
                    ab(0, 128, O_XB2, [[1, 2048]]),
                    ab(0, 128, O_QKV + 2048, [[1, 2048]]), AF.Square),
                    waits=[("vx", vx_sig), ("vx", vx_k)] + fw("XB2"), inc="sc")
                pv = None
                for g4 in range(4):
                    pv = P.do("tensor", lambda t, g4=g4: t.matmul(
                        psT[0:2, g4 * 512:(g4 + 1) * 512], ONB[:, 1:3],
                        ab(0, 128, O_XB2 + g4 * 512, [[1, 512]]),
                        start=True, stop=True),
                        waits=[("sc", sc_ksq)] + (fw("psT") if g4 == 0 else []),
                        inc="pe")
                fmark("XB2", "pe", pv)
                vv = P.do("vector", lambda v: v.reduce_max(
                    CF[0:2, C_M2:C_M2 + 1], psT[0:2, 0:2048], AXX),
                    waits=[("pe", pv)], inc="vx")
                fmark("psT", "vx", vv)
                sv = P.do("scalar", lambda s: s.activation(
                    CF[0:2, C_TMP:C_TMP + 1], CF[0:2, C_M2:C_M2 + 1], AF.Ln),
                    waits=[("vx", vv)], inc="sc")
                sv = P.do("scalar", lambda s: s.activation(
                    CF[0:2, C_M2:C_M2 + 1], CF[0:2, C_TMP:C_TMP + 1], AF.Exp,
                    scale=0.5), waits=[("sc", sv)], inc="sc")
                vv = P.do("vector", lambda v: v.tensor_add(
                    CF[0:2, C_NKE:C_NKE + 1], CF[0:2, C_M2:C_M2 + 1],
                    CF[0:2, C_E2M:C_E2M + 1]),
                    waits=[("sc", sv), ("sc", sc_ne)], inc="vx")
                vv = P.do("vector", lambda v: v.tensor_copy(
                    CB[0:2, CB_NK2:CB_NK2 + 1], CF[0:2, C_NKE:C_NKE + 1]),
                    waits=fw("CBR") + [("vx", vv)], inc="vx")
                pv = P.do("tensor", lambda t: t.matmul(
                    psR[:, 1000:1001], SEL[:], CB[0:2, CB_NK2:CB_NK2 + 1],
                    start=True, stop=True),
                    waits=[("vx", vv), ("sq", sq_pro)] + fw("psR"), inc="pe")
                vv = P.do("vector", lambda v: v.tensor_copy(
                    CF[:, C_NKE:C_NKE + 1], psR[:, 1000:1001]),
                    waits=[("pe", pv)], inc="vx")
                vv = P.do("vector", lambda v: v.tensor_mul(
                    CF[:, C_NK2:C_NK2 + 1], CF[:, C_NKE:C_NKE + 1],
                    CF[:, C_NKE:C_NKE + 1]), waits=[("vx", vv)], inc="vx")
                vx_nk2 = P.do("vector", lambda v: v.tensor_copy(
                    CB[:, 1:2], CF[:, C_NK2:C_NK2 + 1]),
                    waits=[("vx", vv)], inc="vx")
                fmark("psR", "vx", vv)

                for hh in range(2):
                    h = 2 * pr + hh
                    pb = hh * 64
                    ch = h // 2
                    dl = 64 if hh == 0 else 0
                    nr0 = 0 if hh == 0 else 64
                    voff = 0 if hh == 0 else 64

                    pv = None
                    for g in range(4):
                        pv = P.do("tensor", lambda t, g=g, h=h: t.matmul(
                            psT[0:1, g * 512:(g + 1) * 512],
                            ZB[:, h:h + 1],
                            ab(0, 128, O_SIG + g * 512, [[1, 512]]),
                            start=True, stop=True),
                            waits=[("vx", vx_sig)] +
                            (fw("psT") if g == 0 else []), inc="pe")
                    sc_c2 = None
                    for g in range(4):
                        sv = P.do("scalar", lambda s, g=g: s.activation(
                            SF[0:1, 0:512], psT[0:1, g * 512:(g + 1) * 512],
                            AF.Ln, bias=1e-6),
                            waits=[("pe", pv)] +
                            (fw("SBR") if g == 0 else []) +
                            ([("sc", sc_c2)] if sc_c2 else []), inc="sc")
                        sc_c2 = P.do("scalar", lambda s, g=g: s.activation(
                            SB[0:1, g * 512:(g + 1) * 512], SF[0:1, 0:512],
                            AF.Exp, scale=-1.0,
                            bias=CF[0:1, C_LN1MG:C_LN1MG + 1]),
                            waits=[("sc", sv)], inc="sc")
                    pv = None
                    for g in range(4):
                        pv = P.do("tensor", lambda t, g=g: t.matmul(
                            psT[:, g * 512:(g + 1) * 512], ONR[0:1, 0:128],
                            SB[0:1, g * 512:(g + 1) * 512],
                            start=True, stop=True),
                            waits=[("sc", sc_c2)], inc="pe")
                    fmark("SBR", "pe", pv)
                    vx_sigc2 = None
                    for j in range(2):
                        vx_sigc2 = P.do("vector", lambda v, j=j, pb=pb:
                                        v.tensor_mul(
                            ab(pb, 64, O_SIG + j * 1024, [[1, 1024]]),
                            ab(pb, 64, O_SIG + j * 1024, [[1, 1024]]),
                            psT[pb:pb + 64, j * 1024:(j + 1) * 1024]),
                            waits=[("pe", pv)], inc="vx")
                    fmark("psT", "vx", vx_sigc2)

                    pv = None
                    for ti in (range(16) if "ublock" not in skip else []):
                        pv = P.do("tensor", lambda t, ti=ti, pb=pb: t.matmul(
                            psR[:, 1008 + ti:1009 + ti],
                            ab(pb, 64, O_XB + ti * 128, [[1, 128]]),
                            CB[pb:pb + 64, 1:2], start=True, stop=True),
                            waits=[("sc", sc_qsq), ("vx", vx_nk2)] +
                            (fw("psR") if ti == 0 else []), inc="pe")
                    fmark("CBR", "pe", pv)
                    if "ublock" in skip:
                        vx_ub = None
                        sv = None
                    else:
                        sv = P.do("scalar", lambda s: s.activation(
                            UB[:, 0:16], psR[:, 1008:1024], AF.Ln),
                            waits=[("pe", pv)] + fw("UBR"), inc="sc")
                    if "ublock" not in skip:
                        sv = P.do("scalar", lambda s: s.activation(
                            UB[:, 0:16], UB[:, 0:16], AF.Exp, scale=0.5, bias=LN8),
                            waits=[("sc", sv)], inc="sc")
                        vx_ub = P.do("vector", lambda v: v.tensor_scalar(
                            UB[:, 0:16], UB[:, 0:16], -1.0, None, ALU.mult),
                            waits=[("sc", sv)], inc="vx")
                        fclr("UBR")
                        fmark("UBR", "vx", vx_ub)
                        fmark("psR", "sc", sv)

                    pv = None
                    for cc in (range(16) if "vtm" not in skip else []):
                        pv = P.do("tensor", lambda t, cc=cc, pb=pb: t.transpose(
                            psRb[:, cc * 64:(cc + 1) * 64],
                            ab(pb, 64, O_QKV + 2 * 2048 + cc * 128, [[1, 128]]),
                            IDT[pb:pb + 64, pb:pb + 64]),
                            waits=(fw("psR") if cc == 0 else []) +
                            [("vx", vx_qkv)], inc="pe")
                    if "vtm" in skip:
                        vx_vtm = None
                    else:
                        vv = P.do("vector", lambda v, voff=voff: v.tensor_copy(
                            ab(0, 128, O_VTM + voff, [[128, 16], [1, 64]]),
                            psRb[:, 0:1024].rearrange("p (c q) -> p c q", q=64)),
                            waits=[("pe", pv)] + fw("VTM"), inc="vx")
                        fmark("psR", "vx", vv)
                        vx_vtm = vv
                    if "vtm" in dbg and s_i == 0 and h == 0:
                        qv = P.do("sync", lambda s: s.dma_start(
                            taps["vtm"][:], ab(0, 128, O_VTM, [[128, 16], [1, 128]])),
                            waits=[("vx", vx_vtm)], inc="sq", n=16)

                    NT16 = 16
                    exp_done = {}; id_done = {}; c1_done = {}; dg_done = {}
                    qz_done = {}
                    dgr_done = {}; dgev_done = {}; shear_done = {}
                    evA_done = {}; evB_done = {}; evC_done = {}
                    av_done = {}; comb_done = {}

                    def tpo(ti):
                        return O_TP if ti % 2 == 0 else O_TP2

                    def aeo(ti):
                        return O_AEXP if ti % 2 == 0 else O_AEXP2

                    def bso(ti):
                        return O_BSH if ti % 2 == 0 else O_BSH2

                    def emit_T_p12(ti):
                        q0 = ti * 128
                        e0 = 1921 - q0
                        wq = [("vx", vx_q), ("pe", id_done.get(ti - 4))]
                        if ti <= 3:
                            wq += fw("psT")
                        vq = P.do("vector", lambda v, ti=ti, q0=q0, hh=hh:
                                  v.tensor_scalar(
                            ab(0, 128, O_QZ + (ti % 4) * 128, [[1, 128]]),
                            ab(0, 128, O_QKV + q0, [[1, 128]]),
                            QM[:, hh:hh + 1], None, ALU.mult),
                            waits=wq, inc="vx")
                        qz_done[ti] = vq
                        w1 = [("vx", vq)]
                        if ti >= 1:
                            eb = evB_done.get(ti - 1)
                            w1 += [eb if eb else ("vx", None)]
                        else:
                            w1 += fw("psS")
                        pv1 = P.do("tensor", lambda t, ti=ti, e0=e0: t.matmul(
                            psS[:, 0:512],
                            ab(0, 128, O_QZ + (ti % 4) * 128, [[1, 128]]),
                            ab(0, 128, O_E2 + e0, [[1, 512]]),
                            start=True, stop=True), waits=w1, inc="pe")
                        pv2 = P.do("tensor", lambda t, ti=ti, e0=e0: t.matmul(
                            psS[:, 512:1024],
                            ab(0, 128, O_QZ + (ti % 4) * 128, [[1, 128]]),
                            ab(0, 128, O_E2 + e0 + 512, [[1, 512]]),
                            start=True, stop=True), inc="pe")
                        wv = [("pe", pv2)]
                        if ti >= 2:
                            wv.append(("sq", shear_done.get(ti - 2)))
                        ev = P.do("scalar", lambda s, ti=ti: s.copy(
                            ab(0, 128, tpo(ti), [[1, 1024]]), psS[:, 0:1024]),
                            waits=wv, inc="sc")
                        evA_done[ti] = ("sc", ev)

                    def emit_T_p34(ti):
                        q0 = ti * 128
                        e0 = 1921 - q0
                        pv1 = P.do("tensor", lambda t, ti=ti, e0=e0: t.matmul(
                            psS[:, 0:512],
                            ab(0, 128, O_QZ + (ti % 4) * 128, [[1, 128]]),
                            ab(0, 128, O_E2 + e0 + 1024, [[1, 512]]),
                            start=True, stop=True),
                            waits=[evA_done[ti]], inc="pe")
                        pv2 = P.do("tensor", lambda t, ti=ti, e0=e0: t.matmul(
                            psS[:, 512:1024],
                            ab(0, 128, O_QZ + (ti % 4) * 128, [[1, 128]]),
                            ab(0, 128, O_E2 + e0 + 1536, [[1, 512]]),
                            start=True, stop=True), inc="pe")
                        ev = P.do("vector", lambda v, ti=ti: v.tensor_copy(
                            ab(0, 128, tpo(ti) + 1024, [[1, 1024]]), psS[:, 0:1024]),
                            waits=[("pe", pv2)], inc="vx")
                        evB_done[ti] = ("vx", ev)

                    def emit_T_p5(ti):
                        q0 = ti * 128
                        e0 = 1921 - q0
                        w = [("sc", exp_done[ti - 1])] if ti >= 1 else fw("psT")
                        pv1 = P.do("tensor", lambda t, ti=ti, e0=e0: t.matmul(
                            psT[:, 1920:2048],
                            ab(0, 128, O_QZ + (ti % 4) * 128, [[1, 128]]),
                            ab(0, 128, O_E2 + e0 + 2048, [[1, 128]]),
                            start=True, stop=True),
                            waits=w, inc="pe")
                        ev = P.do("vector", lambda v, ti=ti: v.tensor_copy(
                            ab(0, 128, tpo(ti) + 2048, [[1, 128]]),
                            psT[:, 1920:2048]),
                            waits=[("pe", pv1)], inc="vx")
                        evC_done[ti] = ev

                    def emit_shear(ti):
                        w = [evB_done[ti], ("vx", evC_done[ti])]
                        if ti >= 2:
                            w.append(("pe", id_done.get(ti - 2)))
                        if "tp" in dbg and s_i == 0 and h == 0 and ti == 0:
                            P.do("sync", lambda s, ti=ti: s.dma_start(
                                taps["tp"][:],
                                ab(0, 128, tpo(ti), [[1, WTP]])),
                                waits=list(w), inc="sq", n=16)
                        sqv = P.do("sync", lambda s, ti=ti: s.dma_start(
                            ab(0, 128, bso(ti), [[1, 2048]]),
                            AP(AB, tpo(ti) + 127, [[ANE - 1, 128], [1, 2048]])),
                            waits=w, inc="sq", n=16)
                        shear_done[ti] = sqv
                        if "bsh" in dbg and s_i == 0 and h == 0 and ti == 0:
                            P.do("sync", lambda s, ti=ti: s.dma_start(
                                taps["bsh"][:], ab(0, 128, bso(ti), [[1, 2048]])),
                                inc="sq", n=16)

                    def emit_scores(ti):
                        q0 = ti * 128
                        for g2 in range(4):
                            w = []
                            if g2 == 0:
                                w = [("sc", exp_done.get(ti - 1))] if ti >= 1 \
                                    else fw("psT")
                            if g2 == 3:
                                w = [("vx", evC_done.get(ti))]
                            P.do("tensor",
                                lambda t, g2=g2, ti=ti: t.matmul(
                                    psT[:, g2 * 512:(g2 + 1) * 512],
                                    ab(0, 128, O_QZ + (ti % 4) * 128, [[1, 128]]),
                                    ab(0, 128, O_QKV + 2048 + g2 * 512, [[1, 512]]),
                                    start=True, stop=False), waits=w, inc="pe")

                    def emit_ids(ti):
                        lastmm = None
                        for g2 in range(4):
                            w = [("sq", shear_done[ti])] if g2 == 0 else []
                            lastmm = P.do("tensor",
                                lambda t, g2=g2, ti=ti: t.matmul(
                                    psT[:, g2 * 512:(g2 + 1) * 512], IDT[:],
                                    ab(0, 128, bso(ti) + g2 * 512, [[1, 512]]),
                                    start=False, stop=True), waits=w, inc="pe")
                        id_done[ti] = lastmm

                    def emit_exp(ti):
                        sv = P.do("scalar", lambda s, ti=ti: s.activation(
                            ab(0, 128, aeo(ti), [[1, 2048]]),
                            psT[:, 0:2048], AF.Exp,
                            bias=UB[:, ti:ti + 1], scale=0.125,
                            accum_out=CF[:, C_Z:C_Z + 1]),
                            waits=[("pe", dgr_done.get(ti - 2)),
                                   ("pe", id_done[ti]), ("vx", vx_ub)],
                            inc="sc")
                        exp_done[ti] = sv
                        if "aexp" in dbg and s_i == 0 and h == 0 and ti == 0:
                            P.do("sync", lambda s, ti=ti: s.dma_start(
                                taps["aexp"][:], ab(0, 128, aeo(ti), [[1, S]])),
                                waits=[("sc", sv)], inc="sq", n=16)
                        sv = P.do("scalar", lambda s: s.activation(
                            CF[:, C_TMP:C_TMP + 1], CF[:, C_Z:C_Z + 1], AF.Ln),
                            waits=[("sc", sv)], inc="sc")
                        sv = P.do("scalar", lambda s, ti=ti: s.activation(
                            CF[:, c1c(ti):c1c(ti) + 1], CF[:, C_TMP:C_TMP + 1],
                            AF.Exp, scale=-1.0, bias=CF[:, C_LNG:C_LNG + 1]),
                            waits=[("sc", sv),
                                   ("vx", dg_done.get(ti - 2))], inc="sc")
                        c1_done[ti] = sv

                    def emit_dg(ti):
                        vv = P.do("vector", lambda v, ti=ti: v.tensor_scalar(
                            ab(0, 128, O_GS + (ti % 2) * 128, [[1, 128]]),
                            IDT[:], CF[:, c1c(ti):c1c(ti) + 1], None, ALU.mult),
                            waits=[("sc", c1_done[ti]),
                                   ("pe", dgr_done.get(ti - 2))] +
                            (fw("GS") if ti <= 1 else []), inc="vx")
                        dg_done[ti] = vv

                    def emit_dgmm(ti, rnd):
                        pv = None
                        for cc in range(8 * rnd, 8 * rnd + 8):
                            w = []
                            if cc == 8 * rnd:
                                if rnd == 0:
                                    w = [("vx", dg_done[ti]),
                                         ("vx", dgev_done.get((ti - 1, 1)))]
                                    if ti % 4 == 0 and ti >= 4:
                                        w.append(("vx",
                                                  comb_done.get(ti // 4 - 1)))
                                    if ti == 0:
                                        w += [("vx", vx_vtm)] + fw("psR")
                                else:
                                    w = [("vx", dgev_done[(ti, 0)])]
                            pv = P.do("tensor", lambda t, cc=cc, ti=ti: t.matmul(
                                psR[:, (cc % 8) * 128:(cc % 8 + 1) * 128],
                                ab(0, 128, aeo(ti) + cc * 128, [[1, 128]]),
                                ab(0, 128, O_GS + (ti % 2) * 128, [[1, 128]]),
                                start=True, stop=True), waits=w, inc="pe")
                        if rnd == 1:
                            dgr_done[ti] = pv
                        w = [("pe", pv)]
                        if rnd == 0 and ti % 4 == 0 and ti >= 4:
                            w.append(("pe", av_done.get(ti // 4 - 1)))
                        ev = P.do("vector", lambda v, ti=ti, rnd=rnd: v.tensor_copy(
                            ab(0, 128, O_AT + (ti % 4) * 128 + rnd * 8 * 512,
                               [[512, 8], [1, 128]]),
                            psR[:, 0:1024].rearrange("p (c q) -> p c q", q=128)),
                            waits=w, inc="vx")
                        dgev_done[(ti, rnd)] = ev

                    def emit_avglob(g):
                        pv = None
                        for cc in range(16):
                            w = []
                            if cc == 0:
                                w = [("vx", dgev_done[(4 * g + 3, 1)])]
                            pv = P.do("tensor", lambda t, cc=cc: t.matmul(
                                psR[:, 0:512],
                                ab(0, 128, O_VTM + cc * 128, [[1, 128]]),
                                ab(0, 128, O_AT + cc * 512, [[1, 512]]),
                                start=(cc == 0), stop=False),
                                waits=w, inc="pe")
                        pe_gl = P.do("tensor", lambda t, g=g, h=h: t.matmul(
                            psR[:, 0:512],
                            MAU[:, h, :],
                            ab(0, 128, O_SIG + g * 512, [[1, 512]]),
                            start=False, stop=True),
                            waits=[("vx", vx_sigc2), ("gq", gq_pro)], inc="pe")
                        fmark("SIG", "pe", pe_gl)
                        av_done[g] = pe_gl
                        if "combine" in skip:
                            comb_done[g] = P.val["vx"]
                            return P.val["vx"]
                        vv = P.do("vector", lambda v, g=g, nr0=nr0, ch=ch:
                                  v.tensor_copy(
                            combv(nr0, 64, ch, g * 512, 512),
                            psR[nr0:nr0 + 64, 0:512]),
                            waits=[("pe", pe_gl)] + fw("COMB"), inc="vx")
                        comb_done[g] = vv
                        return vv

                    if "tshear" not in skip:
                        emit_T_p12(0)
                        emit_T_p34(0)
                        emit_T_p5(0)
                        emit_shear(0)
                        emit_T_p12(1)

                    for ti in range(NT16 + 1):
                        if ti >= 1 and "transp" not in skip:
                            emit_dgmm(ti - 1, 0)
                        if ti <= NT16 - 1 and "scores" not in skip:
                            emit_scores(ti)
                        if ti >= 1 and "transp" not in skip:
                            emit_dgmm(ti - 1, 1)
                        if ti <= NT16 - 1 and "scores" not in skip:
                            emit_ids(ti)
                        if ti + 1 <= NT16 - 1 and "tshear" not in skip:
                            emit_T_p34(ti + 1)
                        if ti >= 4 and ti % 4 == 0 and "avglob" not in skip:
                            vx_comb = emit_avglob(ti // 4 - 1)
                        if ti <= NT16 - 1 and "scores" not in skip \
                                and "expscale" not in skip:
                            emit_exp(ti)
                            emit_dg(ti)
                        if ti + 1 <= NT16 - 1 and "tshear" not in skip:
                            emit_T_p5(ti + 1)
                            emit_shear(ti + 1)
                        if ti + 2 <= NT16 - 1 and "tshear" not in skip:
                            emit_T_p12(ti + 2)
                    if "avglob" in skip:
                        vx_comb = P.val["vx"]
                    fmark("psR", "vx", vx_comb)
                    if "comb" in dbg and s_i == 0 and h == NH - 1:
                        P.do("sync", lambda s: s.dma_start(
                            taps["comb"][:], ab(0, 128, O_COMB,
                                                [[2048, DC], [1, S]])),
                            waits=[("vx", vx_comb)], inc="sq", n=16)
                    fmark("psT", "sc", exp_done.get(15))
                    fmark("psT", "pe", id_done.get(15))
                    fmark("psS", "vx", evB_done.get(15, (None, None))[1])
                    fmark("psT", "vx", evC_done.get(15))

                fclr("QKV")
                fmark("QKV", "pe", P.val["pe"])
                fmark("QKV", "vx", P.val["vx"])
                fmark("QKV", "sc", P.val["sc"])
                fclr("SIG")
                fmark("SIG", "pe", P.val["pe"])
                fmark("SIG", "vx", P.val["vx"])
                fmark("WO", "pe", P.val["pe"])
                fmark("WO", "vx", P.val["vx"])
                fclr("XB")
                fmark("XB", "pe", P.val["pe"])
            fclr("COMB")
            fmark("COMB", "vx", P.val["vx"])
            if "comb" in dbg and s_i == 0:
                qv = P.do("sync", lambda s: s.dma_start(
                    taps["comb"][:], ab(0, 128, O_COMB, [[2048, DC], [1, S]])),
                    waits=fw("COMB"), inc="sq", n=16)

            if stage == "pairs":
                break
            vx_combv = P.val["vx"]
            wo_dma = {}
            wo_mm = {}
            bank_ev = {}

            def emit_wo_dma(oc):
                wob = O_WO if oc % 2 == 0 else O_WP
                gv = P.do("gpsimd", lambda g, oc=oc, wob=wob: g.dma_start(
                    ab(0, 128, wob, [[128, 8], [1, 128]]),
                    w_d["woT"][:, oc * 128:(oc + 1) * 128].rearrange(
                        "(c p) o -> p c o", p=128)),
                    waits=([("pe", wo_mm.get(oc - 2))] if oc >= 2 else
                           fw("WO", "WP")), inc="gq", n=16)
                wo_dma[oc] = gv

            emit_wo_dma(0)
            emit_wo_dma(1)
            for oc in range(DC):
                wob = O_WO if oc % 2 == 0 else O_WP
                lastmm = None
                for cg in range(4):
                    for dcc in range(DC):
                        wts = []
                        if dcc == 0:
                            wts = [("gq", wo_dma[oc]), ("vx", vx_combv),
                                   ("vx", bank_ev.get((oc - 1, cg)))]
                            if oc == 0 and cg == 0:
                                wts += fw("psT")
                        lastmm = P.do("tensor",
                                      lambda t, cg=cg, dcc=dcc, wob=wob: t.matmul(
                            psT[:, cg * 512:(cg + 1) * 512],
                            ab(0, 128, wob + dcc * 128, [[1, 128]]),
                            ab(0, 128, O_COMB + dcc * 2048 + cg * 512, [[1, 512]]),
                            start=(dcc == 0), stop=(dcc == DC - 1)),
                            waits=wts, inc="pe")
                    vv = P.do("vector", lambda v, oc=oc, cg=cg: v.tensor_scalar(
                        ab(0, 128, O_H1 + oc * 2048 + cg * 512, [[1, 512]]),
                        psT[:, cg * 512:(cg + 1) * 512],
                        BIA[:, 24 + oc:24 + oc + 1], None, ALU.add),
                        waits=[("pe", lastmm)] +
                        (fw("H1") if oc == 0 and cg == 0 else []), inc="vx")
                    bank_ev[(oc, cg)] = vv
                wo_mm[oc] = lastmm
                fmark("COMB", "pe", lastmm)
                if oc + 2 <= DC - 1:
                    emit_wo_dma(oc + 2)
            fclr("WO")
            fmark("WO", "pe", lastmm)
            fmark("WP", "pe", lastmm)
            fmark("psT", "vx", vv)
            vx_y1 = vv
            fclr("H1")
            fmark("H1", "vx", vx_y1)
            if "y1" in dbg and s_i == 0:
                qv = P.do("sync", lambda s: s.dma_start(
                    taps["y1"][:], ab(0, 128, O_H1, [[2048, DC], [1, S]])),
                    waits=[("vx", vx_y1)], inc="sq", n=16)

            sq_xscr_half = {}
            for half in range(2):
                hc0 = half * 1024
                pe_st = emit_ln_stats(
                    lambda c: ab(0, 128, O_H1 + c * 2048 + hc0, [[1, 1024]]),
                    lambda c: [("vx", vx_y1)], True,
                    lambda g2: psT[0:1, g2 * 512:(g2 + 1) * 512],
                    lambda g2: psT[0:1, 1024 + g2 * 512:1024 + (g2 + 1) * 512],
                    src_rg=lambda c: "H1")
                vxr = emit_mrstd(pe_st, psT[0:1, 0:1024], psT[0:1, 1024:2048], 1024)
                fmark("psT", "vx", vxr)
                fmark("XROW", "vx", vxr)
                pvb = []
                for g2 in range(2):
                    pv = emit_bcast(vxr, g2, 0,
                                    psT[:, g2 * 1024:g2 * 1024 + 512],
                                    psT[:, g2 * 1024 + 512:g2 * 1024 + 1024])
                    pvb.append(pv)
                fmark("SBR", "pe", pvb[-1])
                lastsq = None
                for c in range(DC):
                    buf = X_A if c % 2 == 0 else X_B
                    rg = "XA" if c % 2 == 0 else "XBUF"
                    sqv = P.do("sync", lambda s, c=c, buf=buf, s_i=s_i, hc0=hc0:
                               s.dma_start(
                        xf(0, 128, buf, [[1, 1024]]),
                        x_d[s_i, c * 128:(c + 1) * 128, hc0:hc0 + 1024]),
                        waits=fw(rg), inc="sq", n=16)
                    for g2 in range(2):
                        vv = P.do("vector", lambda v, c=c, g2=g2, hc0=hc0: v.tensor_sub(
                            SF[:, 0:512],
                            ab(0, 128, O_H1 + c * 2048 + hc0 + g2 * 512, [[1, 512]]),
                            psT[:, g2 * 1024:g2 * 1024 + 512]),
                            waits=[("pe", pvb[g2])], inc="vx")
                        vv = P.do("vector", lambda v, g2=g2: v.tensor_mul(
                            SF[:, 0:512], SF[:, 0:512],
                            psT[:, g2 * 1024 + 512:g2 * 1024 + 1024]),
                            waits=[("vx", vv)], inc="vx")
                        vv = P.do("vector", lambda v, buf=buf, g2=g2: v.tensor_add(
                            xf(0, 128, buf + g2 * 512, [[1, 512]]),
                            xf(0, 128, buf + g2 * 512, [[1, 512]]),
                            SF[:, 0:512]), waits=[("sq", sqv), ("vx", vv)], inc="vx")
                    sqw = P.do("sync", lambda s, c=c, buf=buf, s_i=s_i, hc0=hc0: s.dma_start(
                        xs_d[s_i, c * 128:(c + 1) * 128, hc0:hc0 + 1024],
                        xf(0, 128, buf, [[1, 1024]])),
                        waits=[("vx", vv)], inc="sq", n=16)
                    fmark(rg, "sq", sqw)
                    lastsq = sqw
                fmark("psT", "vx", vv)
                sq_xscr_half[half] = lastsq
            sq_xscr = lastsq
            phase_mark()

            if stage == "postln":
                break
            for tt in range(NTT):
                tc0 = tt * 512
                sqx_c = {}
                for c in range(DC):
                    sqx_c[c] = P.do("sync",
                        lambda s, tc0=tc0, s_i=s_i, c=c: s.dma_start(
                        xf(0, 128, X_NT + c * 512, [[1, 512]]),
                        xs_d[s_i, c * 128:(c + 1) * 128, tc0:tc0 + 512]),
                        waits=[("sq", sq_xscr_half[tt // 2])] +
                        (fw("XNT") if c == 0 else []), inc="sq", n=16)
                pe_st = emit_ln_stats(
                    lambda c: xf(0, 128, X_NT + c * 512, [[1, 512]]),
                    lambda c: [("sq", sqx_c[c])], False,
                    lambda g2: psT[0:1, 0:512],
                    lambda g2: psT[0:1, 512:1024], width=512,
                    src_rg=lambda c: "XNT")
                vxr = emit_mrstd(pe_st, psT[0:1, 0:512], psT[0:1, 512:1024], 512)
                fmark("psT", "vx", vxr)
                fmark("XROW", "vx", vxr)
                pv = emit_bcast(vxr, 0, 0, psT[:, 1024:1536], psT[:, 1536:2048])
                fmark("SBR", "pe", pv)
                pvb2 = pv
                lastvx = None
                for c in range(DC):
                    vv = P.do("vector", lambda v, c=c: v.tensor_sub(
                        SF[:, 0:512], xf(0, 128, X_NT + c * 512, [[1, 512]]),
                        psT[:, 1024:1536]),
                        waits=[("pe", pvb2)] +
                        (fw("H2", "AEXP", "TP", "BSH") if c == 0 else []),
                        inc="vx")
                    vv = P.do("vector", lambda v, c=c: v.tensor_mul(
                        ab(0, 128, O_FH2 + c * 512, [[1, 512]]),
                        SF[:, 0:512], psT[:, 1536:2048]),
                        waits=[("vx", vv)], inc="vx")
                    lastvx = vv
                vx_h2 = lastvx
                fclr("H2")
                if "h2" in dbg and s_i == 0 and tt == 0:
                    qv = P.do("sync", lambda s: s.dma_start(
                        taps["h2"][:], ab(0, 128, O_FH2, [[512, DC], [1, 512]])),
                        waits=[("vx", vx_h2)], inc="sq", n=16)
                def gu_off(sci):
                    return (O_QKV, O_QKV + 3072) if sci % 2 == 0 else \
                        (50176, 53248)
                gq_gu = {}
                mm_gu = {}

                def emit_gu_dma(sci):
                    og, ou = gu_off(sci)
                    w = fw("FFNW", "QKV") if sci < 2 else \
                        [("pe", mm_gu.get(sci - 2))]
                    gvg = P.do("gpsimd", lambda g, sci=sci, og=og: g.dma_start(
                        ab(0, 128, og, [[384, 8], [1, 384]]),
                        wg_d[:, sci * 384:(sci + 1) * 384].rearrange(
                            "(c p) h -> p c h", p=128)),
                        waits=w, inc="gq", n=16)
                    gvu = P.do("gpsimd", lambda g, sci=sci, ou=ou: g.dma_start(
                        ab(0, 128, ou, [[384, 8], [1, 384]]),
                        wu_d[:, sci * 384:(sci + 1) * 384].rearrange(
                            "(c p) h -> p c h", p=128)), inc="gq", n=16)
                    gq_gu[sci] = gvu

                emit_gu_dma(0)
                emit_gu_dma(1)
                for sci in range(FSUP):
                    og, ou = gu_off(sci)
                    for j in range(3):
                        hc = sci * 3 + j
                        lastg = None
                        for dcc in range(DC):
                            lastg = P.do("tensor", lambda t, j=j, dcc=dcc, og=og:
                                         t.matmul(
                                psS[:, 0:512],
                                ab(0, 128, og + dcc * 384 + j * 128, [[1, 128]]),
                                ab(0, 128, O_FH2 + dcc * 512, [[1, 512]]),
                                start=(dcc == 0), stop=(dcc == DC - 1)),
                                waits=[("gq", gq_gu[sci]), ("vx", vx_h2)] +
                                (fw("psS") if dcc == 0 else []), inc="pe")
                        sv = P.do("scalar", lambda s: s.activation(
                            ab(0, 128, O_GS, [[1, 512]]), psS[:, 0:512],
                            AF.Silu, scale=DQ),
                            waits=[("pe", lastg)] + fw("GS"), inc="sc")
                        lastu = None
                        for dcc in range(DC):
                            lastu = P.do("tensor", lambda t, j=j, dcc=dcc, ou=ou:
                                         t.matmul(
                                psS[:, 512:1024],
                                ab(0, 128, ou + dcc * 384 + j * 128,
                                   [[1, 128]]),
                                ab(0, 128, O_FH2 + dcc * 512, [[1, 512]]),
                                start=(dcc == 0), stop=(dcc == DC - 1)),
                                inc="pe")
                        vv = P.do("vector", lambda v, hc=hc: v.scalar_tensor_tensor(
                            ab(0, 128, O_COMB + hc * 512, [[1, 512]]),
                            psS[:, 512:1024], DQ,
                            ab(0, 128, O_GS, [[1, 512]]), ALU.mult, ALU.mult),
                            waits=[("pe", lastu), ("sc", sv)] + fw("COMB"),
                            inc="vx")
                        fmark("GS", "vx", vv)
                        fmark("psS", "vx", vv)
                    mm_gu[sci] = lastu
                    fmark("FFNW", "pe", lastu)
                    if sci + 2 < FSUP:
                        emit_gu_dma(sci + 2)
                fmark("H2", "pe", lastu)
                vx_h3 = vv
                if "h3" in dbg and s_i == 0 and tt == 0:
                    qv = P.do("sync", lambda s: s.dma_start(
                        taps["h3"][:], ab(0, 128, O_COMB, [[512, FHC], [1, 512]])),
                        waits=[("vx", vx_h3)], inc="sq", n=16)
                def dreg(dcc):
                    if dcc < 2:
                        return psS[:, dcc * 512:(dcc + 1) * 512]
                    if dcc < 6:
                        return psT[:, (dcc - 2) * 512:(dcc - 1) * 512]
                    return psR[:, (dcc - 6) * 512:(dcc - 5) * 512]
                dwaits = fw("psS", "psT", "psR")
                def wd_off(sci):
                    return O_SIG if sci % 2 == 0 else O_WP
                gq_wd = {}
                mm_wd = {}

                def emit_wd_dma(sci):
                    od = wd_off(sci)
                    w = fw("FFNW", "SIG") if sci < 2 else \
                        [("pe", mm_wd.get(sci - 2))]
                    gvd = P.do("gpsimd", lambda g, sci=sci, od=od: g.dma_start(
                        ab(0, 128, od, [[1024, 3], [1, 1024]]),
                        wd_d[sci * 384:(sci + 1) * 384, :].rearrange(
                            "(j p) d -> p j d", p=128)),
                        waits=w, inc="gq", n=16)
                    gq_wd[sci] = gvd

                emit_wd_dma(0)
                emit_wd_dma(1)
                for sci in range(FSUP):
                    od = wd_off(sci)
                    for j in range(3):
                        hc = sci * 3 + j
                        for dcc in range(DC):
                            pv = P.do("tensor", lambda t, j=j, dcc=dcc, hc=hc, od=od:
                                      t.matmul(
                                dreg(dcc),
                                ab(0, 128, od + j * 1024 + dcc * 128, [[1, 128]]),
                                ab(0, 128, O_COMB + hc * 512, [[1, 512]]),
                                start=(hc == 0), stop=(hc == FHC - 1)),
                                waits=[("gq", gq_wd[sci]), ("vx", vx_h3)] +
                                (dwaits if hc == 0 else []), inc="pe")
                    mm_wd[sci] = pv
                    fmark("FFNW", "pe", pv)
                    if sci + 2 < FSUP:
                        emit_wd_dma(sci + 2)
                pe_dn = pv
                fmark("COMB", "pe", pe_dn)
                for dcc in range(DC):
                    vv = P.do("vector", lambda v, dcc=dcc: v.tensor_scalar(
                        ab(0, 128, O_FY2 + dcc * 512, [[1, 512]]),
                        dreg(dcc), DQ, None, ALU.mult),
                        waits=[("pe", pe_dn)] +
                        (fw("Y2", "AT", "BSH") if dcc == 0 else []),
                        inc="vx")
                vx_y2 = vv
                fmark("psS", "vx", vx_y2)
                fmark("psT", "vx", vx_y2)
                fmark("psR", "vx", vx_y2)
                fclr("Y2")
                if "y2" in dbg and s_i == 0 and tt == 0:
                    qv = P.do("sync", lambda s: s.dma_start(
                        taps["y2"][:], ab(0, 128, O_FY2, [[512, DC], [1, 512]])),
                        waits=[("vx", vx_y2)], inc="sq", n=16)
                pe_st = emit_ln_stats(
                    lambda c: ab(0, 128, O_FY2 + c * 512, [[1, 512]]),
                    lambda c: [("vx", vx_y2)], True,
                    lambda g2: psT[0:1, 0:512],
                    lambda g2: psT[0:1, 512:1024], width=512,
                    src_rg=lambda c: "Y2")
                vxr = emit_mrstd(pe_st, psT[0:1, 0:512], psT[0:1, 512:1024], 512)
                fmark("psT", "vx", vxr)
                fmark("XROW", "vx", vxr)
                pv = emit_bcast(vxr, 0, 0, psT[:, 1024:1536], psT[:, 1536:2048])
                fmark("SBR", "pe", pv)
                for c in range(DC):
                    vv = P.do("vector", lambda v, c=c: v.tensor_sub(
                        SF[:, 0:512],
                        ab(0, 128, O_FY2 + c * 512, [[1, 512]]),
                        psT[:, 1024:1536]),
                        waits=[("pe", pv)] + (fw("OUT") if c == 0 else []),
                        inc="vx")
                    vv = P.do("vector", lambda v: v.tensor_mul(
                        SF[:, 0:512], SF[:, 0:512], psT[:, 1536:2048]),
                        waits=[("vx", vv)], inc="vx")
                    vv = P.do("vector", lambda v, c=c: v.tensor_add(
                        xf(0, 128, X_A + c * 512, [[1, 512]]),
                        xf(0, 128, X_NT + c * 512, [[1, 512]]),
                        SF[:, 0:512]), waits=[("vx", vv)], inc="vx")
                fmark("psT", "vx", vv)
                fmark("Y2", "vx", vv)
                fclr("OUT")
                sqo = P.do("sync", lambda s, tc0=tc0, s_i=s_i: s.dma_start(
                    y_d[s_i, :, tc0:tc0 + 512].rearrange("(c p) t -> p c t", p=128),
                    xf(0, 128, X_A, [[512, 8], [1, 512]])),
                    waits=[("vx", vv)], inc="sq", n=16)
                fmark("OUT", "sq", sqo)
                fmark("XA", "sq", sqo)
                fmark("XBUF", "sq", sqo)
                sq_out_total[0] = sqo
                fmark("XNT", "vx", vv)
            phase_mark()

        P.do("gpsimd", lambda g: None, waits=[("sq", P.val["sq"])])
        P.do("sync", lambda s: None, waits=[("gq", P.val["gq"])])

        P.check()

        with nc.Block() as block:
            def mk(eng_name):
                def emit(eng):
                    for ws, fn, inc, n in P.ops[eng_name]:
                        for s, v in ws:
                            eng.wait_ge(sems[s], v)
                        inst = fn(eng)
                        if inc is not None and inst is not None:
                            inst.then_inc(sems[inc], n)
                return emit
            block.tensor(mk("tensor"))
            block.vector(mk("vector"))
            block.scalar(mk("scalar"))
            block.sync(mk("sync"))
            block.gpsimd(mk("gpsimd"))

    return nc, taps


_HOST = {}


def _host_prep(inputs):
    x = np.asarray(inputs["L0_attn_hidden_states"])
    d = {}
    d["wqT"] = np.ascontiguousarray(np.asarray(inputs["L0_attn_Wq"]).T)
    d["wkT"] = np.ascontiguousarray(np.asarray(inputs["L0_attn_Wk"]).T)
    d["wvT"] = np.ascontiguousarray(np.asarray(inputs["L0_attn_Wv"]).T)
    perm = np.empty(D, np.int64)
    for c in range(DC):
        for p in range(128):
            h = 2 * c + (1 if p >= 64 else 0)
            perm[c * 128 + p] = 64 * h + (p % 64)
    d["woT"] = np.ascontiguousarray(np.asarray(inputs["L0_attn_Wo"]).T[:, :])
    d["woT"] = np.ascontiguousarray(d["woT"][perm, :])
    for nm, key in (("bq", "L0_attn_bq"), ("bk", "L0_attn_bk"),
                    ("bv", "L0_attn_bv"), ("bo", "L0_attn_bo")):
        d[nm] = np.ascontiguousarray(
            np.asarray(inputs[key]).reshape(8, 128).T.astype(np.float32))
    rel = np.asarray(inputs["L0_attn_rel_emb"])
    m_idx = 2048 - np.clip(np.arange(EPW) - 1024, 0, 2048)
    e2h = np.ascontiguousarray(rel[m_idx].T.astype(np.float32))
    d["e2d"] = np.ascontiguousarray(np.concatenate([e2h, e2h], axis=0))
    M = np.asarray(inputs["L0_attn_M"]); z = np.asarray(inputs["L0_attn_z"])
    ma = np.zeros((128, NH, 128), np.float32)
    for h in range(NH):
        if h % 2 == 0:
            ma[0:64, h, 0:64] = M[h]; ma[0:64, h, 64] = z[h]
        else:
            ma[64:128, h, 64:128] = M[h]; ma[64:128, h, 0] = z[h]
    d["maugd"] = ma
    zb = np.zeros((128, NH), np.float32)
    for h in range(NH):
        p0 = (h % 2) * 64
        zb[p0:p0 + 64, h] = z[h]
    d["zbd"] = zb.astype(ml_dtypes.bfloat16)
    d["wgT"] = np.ascontiguousarray(np.asarray(inputs["L0_ffn_w_gate"]).T)
    d["wuT"] = np.ascontiguousarray(np.asarray(inputs["L0_ffn_w_up"]).T)
    d["wdT"] = np.ascontiguousarray(np.asarray(inputs["L0_ffn_w_down"]).T)
    d["galpha"] = np.asarray(inputs["L0_attn_gate_alpha"],
                             np.float32).reshape(1, 1)
    d["idd"] = np.eye(128, dtype=ml_dtypes.bfloat16)
    sel = np.zeros((2, 128), np.float32)
    sel[0, 0:64] = 1.0; sel[1, 64:128] = 1.0
    d["seld"] = sel.astype(ml_dtypes.bfloat16)
    xs = []
    for c in range(8):
        xc = np.ascontiguousarray(
            x[2 * c:2 * c + 2].transpose(0, 2, 1)).astype(np.float32)
        xs.append(xc)
    return d, xs


def kernel(**inputs):
    shared, xs = _host_prep(inputs)
    nc, _taps = build_program()
    in_maps = []
    for c in range(8):
        m = {"x": xs[c]}
        m.update(shared)
        in_maps.append(m)
    res = run_bass_kernel_spmd(nc, in_maps, core_ids=list(range(8)))
    out = np.empty((16, S, D), np.float32)
    for c in range(8):
        y = res.results[c]["yout"]
        out[2 * c] = y[0].T
        out[2 * c + 1] = y[1].T
    return out


# version 22
# speedup vs baseline: 1.0529x; 1.0529x over previous
import sys
sys.path.insert(0, "/opt/trn_rl_repo")
import numpy as np
import ml_dtypes
import concourse.bass as bass
import concourse.mybir as mybir
from concourse.bass_utils import run_bass_kernel_spmd
from concourse.ap import AP
from contextlib import ExitStack
import concourse.bass_utils as _bu

if not getattr(_bu, "_ldwopt_patched", False):
    _orig_run_command = _bu.run_command

    def _patched_run_command(cmd, *a, **kw):
        if isinstance(cmd, list):
            cmd = [c.replace("--enable-ldw-opt=false", "--enable-ldw-opt=false")
                   if isinstance(c, str) else c for c in cmd]
        return _orig_run_command(cmd, *a, **kw)
    _bu.run_command = _patched_run_command
    _bu._ldwopt_patched = True

F32 = mybir.dt.float32
BF16 = mybir.dt.bfloat16
I32 = mybir.dt.int32
AF = mybir.ActivationFunctionType
ALU = mybir.AluOpType
AXX = mybir.AxisListType.X

S = 2048
D = 1024
NH = 16
NPAIR = 8
NSEQ = 2
DC = 8
FH = 3072
FHC = 24
FSUP = 8
NTT = 4
WTP = 2176
EPW = 4097
EPS = 1e-5
DQ = 1.0 / 127.0

O_H1 = 0
O_COMB = 16384
O_QKV = 32768
O_SIG = 38912
O_AEXP = 40960
O_FH2 = 41984
O_TP = 43008
O_BSH = 45184
O_FY2 = 46080
O_AT = 47232
O_GS = 73985
O_VTM = 55424
O_XB = 57472
O_XB2 = 59520
O_E2 = 61568
O_WP = 65665
O_WO = 68737
O_QZ = O_WO
O_TP2 = 69761
O_BSH2 = 71937
O_AEXP2 = 74497
ANE = 76545

X_A = 0
X_B = 1024
X_ROW = 2048
X_NT = 5120
XNE = 9216

ENGS = ("tensor", "vector", "scalar", "sync", "gpsimd")


class _B:
    def __init__(self, sems):
        self.sems = sems
        self.owner = {"pe": "tensor", "vx": "vector", "sc": "scalar",
                      "sq": "sync", "gq": "gpsimd"}
        self.val = {k: 0 for k in sems}
        self.ops = {e: [] for e in ENGS}
        self.lastw = {e: {} for e in ENGS}

    def do(self, eng, fn, waits=(), inc=None, n=1):
        ws = []
        for (s, v) in waits:
            if v is None or v <= 0:
                continue
            if self.lastw[eng].get(s, 0) >= v:
                continue
            self.lastw[eng][s] = v
            ws.append((s, v))
        if inc is not None:
            self.val[inc] += n
        self.ops[eng].append((ws, fn, inc, n))
        return self.val[inc] if inc is not None else None

    def check(self):
        idx = {e: 0 for e in ENGS}
        cur = {k: 0 for k in self.val}
        progress = True
        while progress:
            progress = False
            for e in ENGS:
                while idx[e] < len(self.ops[e]):
                    ws, fn, inc, n = self.ops[e][idx[e]]
                    if any(cur[s] < v for (s, v) in ws):
                        break
                    if inc is not None:
                        cur[inc] += n
                    idx[e] += 1
                    progress = True
        stuck = {e: (idx[e], len(self.ops[e])) for e in ENGS
                 if idx[e] < len(self.ops[e])}
        if stuck:
            msg = []
            for e, (i, t) in stuck.items():
                msg.append(f"{e}@{i}/{t} waits {self.ops[e][i][0]}")
            raise RuntimeError("static deadlock: " + "; ".join(msg))


def build_program(dbg=frozenset(), npair=NPAIR, nseq=NSEQ, stage="full", skip=frozenset()):
    nc = bass.Bass("TRN2", target_bir_lowering=False)
    x_d = nc.dram_tensor("x", [NSEQ, D, S], F32, kind="ExternalInput")
    w_d = {nm: nc.dram_tensor(nm, [D, D], F32, kind="ExternalInput")
           for nm in ("wqT", "wkT", "wvT", "woT")}
    b_d = {nm: nc.dram_tensor(nm, [128, 8], F32, kind="ExternalInput")
           for nm in ("bq", "bk", "bv", "bo")}
    e2_d = nc.dram_tensor("e2d", [128, EPW], F32, kind="ExternalInput")
    ma_d = nc.dram_tensor("maugd", [128, NH, 128], F32, kind="ExternalInput")
    wg_d = nc.dram_tensor("wgT", [D, FH], I32, kind="ExternalInput")
    wu_d = nc.dram_tensor("wuT", [D, FH], I32, kind="ExternalInput")
    wd_d = nc.dram_tensor("wdT", [FH, D], I32, kind="ExternalInput")
    ga_d = nc.dram_tensor("galpha", [1, 1], F32, kind="ExternalInput")
    zb_d = nc.dram_tensor("zbd", [128, 16], BF16, kind="ExternalInput")
    id_d = nc.dram_tensor("idd", [128, 128], BF16, kind="ExternalInput")
    sel_d = nc.dram_tensor("seld", [2, 128], BF16, kind="ExternalInput")
    xs_d = nc.dram_tensor("xscr", [NSEQ, D, S], F32, kind="Internal")
    y_d = nc.dram_tensor("yout", [NSEQ, D, S], F32, kind="ExternalOutput")

    taps = {}
    tapspec = {"h1": ([128, DC, S], BF16), "qkv": ([128, 3, S], BF16),
               "tp": ([128, WTP], BF16), "bsh": ([128, S], BF16),
               "aexp": ([128, S], BF16), "cols": ([128, 32], F32),
               "comb": ([128, DC, S], BF16), "y1": ([128, DC, S], BF16),
               "sig": ([128, S], BF16), "at": ([128, 16, 512], BF16),
               "vtm": ([128, 16, 128], BF16), "h2": ([128, DC, 512], BF16),
               "gdbg": ([128, 2048], F32), "gdb2": ([128, 1024], F32),
               "qkv3": ([128, 3, S], BF16), "sig3": ([128, S], BF16),
               "h3": ([128, FHC, 512], BF16), "y2": ([128, DC, 512], BF16)}
    for t in dbg:
        sh, dt_ = tapspec[t]
        taps[t] = nc.dram_tensor("tap_" + t, sh, dt_, kind="ExternalOutput")

    with ExitStack() as st:
        AB = st.enter_context(nc.sbuf_tensor("AB", [128, ANE], BF16))
        XF = st.enter_context(nc.sbuf_tensor("XF", [128, XNE], F32))
        SF = st.enter_context(nc.sbuf_tensor("SF", [128, 512], F32))
        SB = st.enter_context(nc.sbuf_tensor("SB", [128, 2048], BF16))
        UB = st.enter_context(nc.sbuf_tensor("UB", [128, 16], F32))
        CF = st.enter_context(nc.sbuf_tensor("CF", [128, 32], F32))
        CB = st.enter_context(nc.sbuf_tensor("CB", [128, 16], BF16))
        BIA = st.enter_context(nc.sbuf_tensor("BIA", [128, 32], F32))
        MAU = st.enter_context(nc.sbuf_tensor("MAU", [128, NH, 128], BF16))
        IDT = st.enter_context(nc.sbuf_tensor("IDT", [128, 128], BF16))
        ONB = st.enter_context(nc.sbuf_tensor("ONB", [128, 4], BF16))
        ONR = st.enter_context(nc.sbuf_tensor("ONR", [128, 128], BF16))
        ON32 = st.enter_context(nc.sbuf_tensor("ON32", [1, 128], F32))
        SEL = st.enter_context(nc.sbuf_tensor("SEL", [2, 128], BF16))
        ZB = st.enter_context(nc.sbuf_tensor("ZB", [128, 16], BF16))
        QM = st.enter_context(nc.sbuf_tensor("QM", [128, 2], F32))
        GA = st.enter_context(nc.sbuf_tensor("GA", [1, 1], F32))
        CONSTS = st.enter_context(nc.sbuf_tensor("CONSTS", [128, 10], F32))
        psT = st.enter_context(nc.psum_tensor("psT", [128, 2048], F32))
        psS = st.enter_context(nc.psum_tensor("psS", [128, 1024], F32))
        psR = st.enter_context(nc.psum_tensor("psR", [128, 1024], F32))
        psRb = psR[:].bitcast(BF16)
        sems = {k: st.enter_context(nc.semaphore(k))
                for k in ("pe", "vx", "sc", "sq", "gq")}
        P = _B(sems)

        def ab(p0, np_, off, dims):
            return AP(AB, p0 * ANE + off, [[ANE, np_]] + [list(dd) for dd in dims])

        def xf(p0, np_, off, dims):
            return AP(XF, p0 * XNE + off, [[XNE, np_]] + [list(dd) for dd in dims])

        def combv(p0, np_, dc, c0, n):
            return ab(p0, np_, O_COMB + dc * 2048 + c0, [[1, n]])

        (C_Z0, C_Z1, C_Z, C_C1, C_G, C_G1M, C_NKE, C_E2M, C_M2, C_TMP,
         C_NK2, C_LNG, C_LN1MG, C_C1B) = range(14)
        CB_NK2 = 0

        def c1c(ti):
            return C_C1 if ti % 2 == 0 else C_C1B

        free = {k: {} for k in
                ("psT", "psS", "psR", "TP", "BSH", "AEXP", "AT", "VTM", "QKV",
                 "SIG", "H1", "COMB", "WP", "WO", "XA", "XBUF", "XB", "XB2",
                 "SBR", "UBR", "CBR", "FFNW", "GS", "H2", "Y2", "OUT", "XROW",
                 "XNT")}

        def fmark(rg, sem, v):
            if v is not None and free[rg].get(sem, 0) < v:
                free[rg][sem] = v

        def fclr(rg):
            free[rg] = {}

        def fw(*rgs):
            out = {}
            for rg in rgs:
                for s, v in free[rg].items():
                    if out.get(s, 0) < v:
                        out[s] = v
            return list(out.items())

        LN8 = float(np.log(1.0 / 8.0))

        vv = None
        for f in (lambda v: v.memset(ab(0, 128, O_VTM, [[1, 2048]]), 0.0),
                  lambda v: v.memset(CF[:], 0.0),
                  lambda v: v.memset(CB[:], 0.0),
                  lambda v: v.memset(UB[:], 0.0),
                  lambda v: v.memset(ONB[:], 0.0),
                  lambda v: v.memset(ONB[:, 0:1], 1.0),
                  lambda v: v.memset(ONB[0:64, 1:2], 1.0),
                  lambda v: v.memset(ONB[64:128, 2:3], 1.0),
                  lambda v: v.memset(ONR[:], 1.0),
                  lambda v: v.memset(ON32[:], 1.0),
                  lambda v: v.memset(CF[0:2, C_E2M:C_E2M + 1], 0.0),
                  lambda v: v.memset(QM[:], 0.0),
                  lambda v: v.memset(QM[0:64, 0:1], 1.0),
                  lambda v: v.memset(QM[64:128, 1:2], 1.0)):
            vv = P.do("vector", f, waits=([("vx", vv)] if vv else []), inc="vx")
        _cvals = [0.0, 1.0, -1.0, 0.125, 0.5, LN8, EPS, DQ, 1e-6, -0.5]
        for j, cv in enumerate(_cvals):
            vv = P.do("vector", lambda v, j=j, cv=cv: v.memset(
                CONSTS[:, j:j + 1], cv), inc="vx")
            nc.const_aps.register(F32, cv, CONSTS[:, j:j + 1])
        v_consts = vv

        qv = P.do("sync", lambda s: s.dma_start(IDT[:], id_d[:]), inc="sq", n=16)
        qv = P.do("sync", lambda s: s.dma_start(SEL[:], sel_d[:]), inc="sq", n=16)
        for i, nm in enumerate(("bq", "bk", "bv", "bo")):
            qv = P.do("sync", lambda s, nm=nm, i=i: s.dma_start(
                BIA[:, i * 8:(i + 1) * 8], b_d[nm][:]), inc="sq", n=16)
        qv = P.do("sync", lambda s: s.dma_start(GA[:], ga_d[:]), inc="sq", n=16)
        qv = P.do("sync", lambda s: s.dma_start(ZB[:], zb_d[:]), inc="sq", n=16)
        sq_pro = qv
        gv = P.do("gpsimd", lambda g: g.dma_start(
            ab(0, 128, O_E2, [[1, EPW]]), e2_d[:]), inc="gq", n=16)
        gv = P.do("gpsimd", lambda g: g.dma_start(MAU[:], ma_d[:]), inc="gq", n=16)
        gq_pro = gv

        sv = P.do("scalar", lambda s: s.activation(GA[:], GA[:], AF.Sigmoid),
                  waits=[("sq", sq_pro)], inc="sc")
        pv = P.do("tensor", lambda t: t.matmul(psS[:, 0:1], ON32[:], GA[:],
                                               start=True, stop=True),
                  waits=[("sc", sv), ("vx", v_consts)], inc="pe")
        vv = P.do("vector", lambda v: v.tensor_copy(CF[:, C_G:C_G + 1], psS[:, 0:1]),
                  waits=[("pe", pv)], inc="vx")
        vv = P.do("vector", lambda v: v.tensor_scalar(
            CF[:, C_G1M:C_G1M + 1], CF[:, C_G:C_G + 1], -1.0, 1.0, ALU.mult, ALU.add),
            waits=[("vx", vv)], inc="vx")
        fmark("psS", "vx", vv)
        sv = P.do("scalar", lambda s: s.activation(
            CF[:, C_LNG:C_LNG + 1], CF[:, C_G:C_G + 1], AF.Ln),
            waits=[("vx", vv)], inc="sc")
        sv = P.do("scalar", lambda s: s.activation(
            CF[:, C_LN1MG:C_LN1MG + 1], CF[:, C_G1M:C_G1M + 1], AF.Ln),
            waits=[("sc", sv)], inc="sc")

        for pc in range((EPW + 511) // 512):
            c0 = pc * 512
            n = min(512, EPW - c0)
            sv = P.do("scalar", lambda s, c0=c0, n=n: s.activation(
                ab(0, 128, O_XB, [[1, n]]), ab(0, 128, O_E2 + c0, [[1, n]]),
                AF.Square), waits=[("gq", gq_pro)] + fw("XB"), inc="sc")
            pv = P.do("tensor", lambda t, n=n: t.matmul(
                psS[0:2, 0:n], ONB[:, 1:3], ab(0, 128, O_XB, [[1, n]]),
                start=True, stop=True),
                waits=[("sc", sv)] + fw("psS"), inc="pe")
            fmark("XB", "pe", pv)
            vv = P.do("vector", lambda v, n=n: v.reduce_max(
                CF[0:2, C_TMP:C_TMP + 1], psS[0:2, 0:n], AXX),
                waits=[("pe", pv), ("vx", v_consts)], inc="vx")
            vv = P.do("vector", lambda v: v.tensor_max(
                CF[0:2, C_E2M:C_E2M + 1], CF[0:2, C_E2M:C_E2M + 1],
                CF[0:2, C_TMP:C_TMP + 1]), waits=[("vx", vv)], inc="vx")
            fmark("psS", "vx", vv)
        sv = P.do("scalar", lambda s: s.activation(
            CF[0:2, C_TMP:C_TMP + 1], CF[0:2, C_E2M:C_E2M + 1], AF.Ln),
            waits=[("vx", vv)], inc="sc")
        sc_ne = P.do("scalar", lambda s: s.activation(
            CF[0:2, C_E2M:C_E2M + 1], CF[0:2, C_TMP:C_TMP + 1], AF.Exp, scale=0.5),
            waits=[("sc", sv)], inc="sc")

        sq_out_total = [0]

        def phase_mark():
            for rg in free:
                for s in ("pe", "vx", "sc", "sq", "gq"):
                    fmark(rg, s, P.val[s])

        def emit_ln_stats(src_fn, wsrc_fn, bf16src, psum_sx, psum_sx2, nchunks=DC,
                          width=1024, src_rg=None):
            lastpe = None
            for c in range(nchunks):
                srcap = src_fn(c)
                wsrc = wsrc_fn(c)
                if bf16src:
                    xbap = srcap
                    vxv = None
                else:
                    vxv = P.do("vector", lambda v, srcap=srcap, width=width:
                               v.tensor_copy(
                        ab(0, 128, O_XB, [[1, width]]), srcap),
                        waits=wsrc + fw("XB"), inc="vx")
                    xbap = ab(0, 128, O_XB, [[1, width]])
                scv = P.do("scalar", lambda s, srcap=srcap, width=width:
                           s.activation(
                    ab(0, 128, O_XB2, [[1, width]]), srcap, AF.Square),
                    waits=wsrc + fw("XB2"), inc="sc")
                ng = width // 512
                for g2 in range(ng):
                    wssx = ([("vx", vxv)] if vxv else wsrc) + \
                        (fw("psT", "psS", "psR") if c == 0 and g2 == 0 else [])
                    pv = P.do("tensor", lambda t, g2=g2, c=c, xbap=xbap: t.matmul(
                        psum_sx(g2), ONB[:, 0:1],
                        AP(xbap.tensor, xbap.offset + g2 * 512,
                           [list(d) for d in xbap.ap[:-1]] + [[1, 512]]),
                        start=(c == 0), stop=(c == nchunks - 1)),
                        waits=wssx, inc="pe")
                    pv = P.do("tensor", lambda t, g2=g2, c=c: t.matmul(
                        psum_sx2(g2), ONB[:, 0:1],
                        ab(0, 128, O_XB2 + g2 * 512, [[1, 512]]),
                        start=(c == 0), stop=(c == nchunks - 1)),
                        waits=[("sc", scv)], inc="pe")
                lastpe = pv
                if src_rg is not None:
                    rg = src_rg(c)
                    if vxv is not None:
                        fmark(rg, "vx", vxv)
                    fmark(rg, "sc", scv)
                    if bf16src:
                        fmark(rg, "pe", pv)
                if not bf16src:
                    fmark("XB", "pe", lastpe)
                fmark("XB2", "pe", lastpe)
            return lastpe

        def emit_mrstd(pe_stats, sx_ap, sx2_ap, width, nd=D):
            m_ap = xf(0, 1, X_ROW, [[1, width]])
            v_ap = xf(0, 1, X_ROW + 1024, [[1, width]])
            t_ap = xf(0, 1, X_ROW + 2048, [[1, width]])
            wv = fw("XROW")
            vv = P.do("vector", lambda v: v.tensor_scalar(
                m_ap, sx_ap, 1.0 / nd, None, ALU.mult),
                waits=[("pe", pe_stats)] + wv, inc="vx")
            vv = P.do("vector", lambda v: v.tensor_scalar(
                v_ap, sx2_ap, 1.0 / nd, None, ALU.mult),
                waits=[("vx", vv)], inc="vx")
            vv = P.do("vector", lambda v: v.tensor_mul(t_ap, m_ap, m_ap),
                      waits=[("vx", vv)], inc="vx")
            vv = P.do("vector", lambda v: v.tensor_sub(v_ap, v_ap, t_ap),
                      waits=[("vx", vv)], inc="vx")
            sv = P.do("scalar", lambda s: s.activation(t_ap, v_ap, AF.Ln,
                                                       bias=EPS),
                      waits=[("vx", vv)], inc="sc")
            sv = P.do("scalar", lambda s: s.activation(v_ap, t_ap, AF.Exp,
                                                       scale=-0.5),
                      waits=[("sc", sv)], inc="sc")
            fmark("XROW", "sc", sv)
            wsb = fw("SBR")
            vv1 = P.do("vector", lambda v: v.tensor_copy(SB[0:1, 0:width], m_ap),
                       waits=wsb + [("vx", vv), ("sc", sv)], inc="vx")
            vv2 = P.do("vector", lambda v: v.tensor_copy(
                SB[0:1, 1024:1024 + width], v_ap), waits=[("vx", vv1)], inc="vx")
            return vv2

        def emit_bcast(vx_rows, g2, mcol0, psum_m, psum_r):
            pv = P.do("tensor", lambda t: t.matmul(
                psum_m, ONR[0:1, :], SB[0:1, mcol0 + g2 * 512:mcol0 + g2 * 512 + 512],
                start=True, stop=True), waits=[("vx", vx_rows)] + fw("psT"),
                inc="pe")
            pv = P.do("tensor", lambda t: t.matmul(
                psum_r, ONR[0:1, :],
                SB[0:1, 1024 + mcol0 + g2 * 512:1024 + mcol0 + g2 * 512 + 512],
                start=True, stop=True), inc="pe")
            return pv

        for s_i in range(nseq):
            for half in range(2):
                hc0 = half * 1024
                xsrc = {}

                def ln1src(c, hc0=hc0, xsrc=xsrc, s_i=s_i):
                    buf = X_A if c % 2 == 0 else X_B
                    rg = "XA" if c % 2 == 0 else "XBUF"
                    sqv = P.do("sync", lambda s, c=c, buf=buf, s_i=s_i, hc0=hc0: s.dma_start(
                        xf(0, 128, buf, [[1, 1024]]),
                        x_d[s_i, c * 128:(c + 1) * 128, hc0:hc0 + 1024]),
                        waits=fw(rg), inc="sq", n=16)
                    xsrc[c] = sqv
                    return xf(0, 128, buf, [[1, 1024]])

                pe_st = emit_ln_stats(
                    ln1src, lambda c: [("sq", xsrc[c])], False,
                    lambda g2: psT[0:1, g2 * 512:(g2 + 1) * 512],
                    lambda g2: psT[0:1, 1024 + g2 * 512:1024 + (g2 + 1) * 512],
                    src_rg=lambda c: "XA" if c % 2 == 0 else "XBUF")
                vxr = emit_mrstd(pe_st, psT[0:1, 0:1024], psT[0:1, 1024:2048], 1024)
                fmark("psT", "vx", vxr)
                fmark("XROW", "vx", vxr)
                pvb = []
                for g2 in range(2):
                    pv = emit_bcast(vxr, g2, 0,
                                    psT[:, g2 * 1024:g2 * 1024 + 512],
                                    psT[:, g2 * 1024 + 512:g2 * 1024 + 1024])
                    pvb.append(pv)
                fmark("SBR", "pe", pvb[-1])
                lastvx = None
                for c in range(DC):
                    buf = X_A if c % 2 == 0 else X_B
                    rg = "XA" if c % 2 == 0 else "XBUF"
                    sqv = P.do("sync", lambda s, c=c, buf=buf, s_i=s_i, hc0=hc0: s.dma_start(
                        xf(0, 128, buf, [[1, 1024]]),
                        x_d[s_i, c * 128:(c + 1) * 128, hc0:hc0 + 1024]),
                        waits=fw(rg), inc="sq", n=16)
                    for g2 in range(2):
                        vv = P.do("vector", lambda v, buf=buf, g2=g2: v.tensor_sub(
                            SF[:, 0:512], xf(0, 128, buf + g2 * 512, [[1, 512]]),
                            psT[:, g2 * 1024:g2 * 1024 + 512]),
                            waits=[("sq", sqv), ("pe", pvb[g2])] + fw("H1"),
                            inc="vx")
                        vv = P.do("vector", lambda v, c=c, g2=g2, hc0=hc0: v.tensor_mul(
                            ab(0, 128, O_H1 + c * 2048 + hc0 + g2 * 512, [[1, 512]]),
                            SF[:, 0:512],
                            psT[:, g2 * 1024 + 512:g2 * 1024 + 1024]),
                            waits=[("vx", vv)], inc="vx")
                    fmark(rg, "vx", vv)
                    lastvx = vv
                fmark("psT", "vx", lastvx)
                fclr("H1")
                fmark("H1", "vx", lastvx)
            vx_h1 = lastvx
            if "h1" in dbg and s_i == 0:
                qv = P.do("sync", lambda s: s.dma_start(
                    taps["h1"][:], ab(0, 128, O_H1, [[2048, DC], [1, S]])),
                    waits=[("vx", vx_h1)], inc="sq", n=16)

            if stage == "ln1":
                break
            for pr in range(npair):
                gws = []
                for wi, wnm in enumerate(("wqT", "wkT", "wvT")):
                    gv = P.do("gpsimd", lambda g, wi=wi, wnm=wnm, pr=pr: g.dma_start(
                        ab(0, 128, O_WP + wi * 1024, [[128, 8], [1, 128]]),
                        w_d[wnm][:, pr * 128:(pr + 1) * 128].rearrange(
                            "(c p) o -> p c o", p=128)),
                        waits=fw("WP"), inc="gq", n=16)
                    gws.append(gv)
                qkv_ev = {}
                for wi in range(3):
                    lastmm = None
                    for cg in range(4):
                        for dcc in range(DC):
                            wts = []
                            if dcc == 0:
                                wts = [("gq", gws[wi]), ("vx", vx_h1),
                                       ("vx", qkv_ev.get((wi - 1, cg)))]
                                if wi == 0 and cg == 0:
                                    wts += fw("psT")
                            lastmm = P.do("tensor", lambda t, cg=cg, dcc=dcc, wi=wi:
                                          t.matmul(
                                psT[:, cg * 512:(cg + 1) * 512],
                                ab(0, 128, O_WP + wi * 1024 + dcc * 128, [[1, 128]]),
                                ab(0, 128, O_H1 + dcc * 2048 + cg * 512, [[1, 512]]),
                                start=(dcc == 0), stop=(dcc == DC - 1)),
                                waits=wts, inc="pe")
                        vv = P.do("vector", lambda v, wi=wi, pr=pr, cg=cg:
                                  v.tensor_scalar(
                            ab(0, 128, O_QKV + wi * 2048 + cg * 512, [[1, 512]]),
                            psT[:, cg * 512:(cg + 1) * 512],
                            BIA[:, wi * 8 + pr:wi * 8 + pr + 1], None,
                            ALU.add), waits=[("pe", lastmm)] +
                            (fw("QKV") if wi == 0 and cg == 0 else []), inc="vx")
                        qkv_ev[(wi, cg)] = vv
                fmark("psT", "vx", vv)
                vx_qkv = vv
                vx_q = qkv_ev[(0, 3)]
                vx_k = qkv_ev[(1, 3)]
                fmark("WP", "pe", lastmm)
                if "qkv3" in dbg and s_i == 0 and pr == 3:
                    qv = P.do("sync", lambda s: s.dma_start(
                        taps["qkv3"][:], ab(0, 128, O_QKV, [[2048, 3], [1, S]])),
                        waits=[("vx", vx_qkv)], inc="sq", n=16)
                if "qkv" in dbg and s_i == 0 and pr == 0:
                    qv = P.do("sync", lambda s: s.dma_start(
                        taps["qkv"][:], ab(0, 128, O_QKV, [[2048, 3], [1, S]])),
                        waits=[("vx", vx_qkv)], inc="sq", n=16)

                sc_qsq = P.do("scalar", lambda s: s.activation(
                    ab(0, 128, O_XB, [[1, 2048]]),
                    ab(0, 128, O_QKV, [[1, 2048]]), AF.Square),
                    waits=[("vx", vx_q)] + fw("XB"), inc="sc")
                vv = P.do("vector", lambda v: v.tensor_scalar(
                    ab(0, 128, O_XB2, [[1, 2048]]),
                    ab(0, 128, O_QKV, [[1, 2048]]), 0.0, None, ALU.min),
                    waits=[("vx", vx_q)] + fw("XB2"), inc="vx")
                sv = P.do("scalar", lambda s: s.activation(
                    ab(0, 128, O_SIG, [[1, 2048]]),
                    ab(0, 128, O_XB2, [[1, 2048]]), AF.Exp),
                    waits=[("vx", vv)] + fw("SIG"), inc="sc")
                vx_sig = P.do("vector", lambda v: v.scalar_tensor_tensor(
                    ab(0, 128, O_SIG, [[1, 2048]]),
                    ab(0, 128, O_QKV, [[1, 2048]]), 0.0,
                    ab(0, 128, O_SIG, [[1, 2048]]), ALU.max, ALU.add),
                    waits=[("sc", sv)], inc="vx")
                fmark("XB2", "sc", sv)
                fmark("XB2", "vx", vx_sig)
                if "sig3" in dbg and s_i == 0 and pr == 3:
                    qv = P.do("sync", lambda s: s.dma_start(
                        taps["sig3"][:], ab(0, 128, O_SIG, [[1, S]])),
                        waits=[("vx", vx_sig)], inc="sq", n=16)
                if "sig" in dbg and s_i == 0 and pr == 0:
                    qv = P.do("sync", lambda s: s.dma_start(
                        taps["sig"][:], ab(0, 128, O_SIG, [[1, S]])),
                        waits=[("vx", vx_sig)], inc="sq", n=16)
                sc_ksq = P.do("scalar", lambda s: s.activation(
                    ab(0, 128, O_XB2, [[1, 2048]]),
                    ab(0, 128, O_QKV + 2048, [[1, 2048]]), AF.Square),
                    waits=[("vx", vx_sig), ("vx", vx_k)] + fw("XB2"), inc="sc")
                pv = None
                for g4 in range(4):
                    pv = P.do("tensor", lambda t, g4=g4: t.matmul(
                        psT[0:2, g4 * 512:(g4 + 1) * 512], ONB[:, 1:3],
                        ab(0, 128, O_XB2 + g4 * 512, [[1, 512]]),
                        start=True, stop=True),
                        waits=[("sc", sc_ksq)] + (fw("psT") if g4 == 0 else []),
                        inc="pe")
                fmark("XB2", "pe", pv)
                vv = P.do("vector", lambda v: v.reduce_max(
                    CF[0:2, C_M2:C_M2 + 1], psT[0:2, 0:2048], AXX),
                    waits=[("pe", pv)], inc="vx")
                fmark("psT", "vx", vv)
                sv = P.do("scalar", lambda s: s.activation(
                    CF[0:2, C_TMP:C_TMP + 1], CF[0:2, C_M2:C_M2 + 1], AF.Ln),
                    waits=[("vx", vv)], inc="sc")
                sv = P.do("scalar", lambda s: s.activation(
                    CF[0:2, C_M2:C_M2 + 1], CF[0:2, C_TMP:C_TMP + 1], AF.Exp,
                    scale=0.5), waits=[("sc", sv)], inc="sc")
                vv = P.do("vector", lambda v: v.tensor_add(
                    CF[0:2, C_NKE:C_NKE + 1], CF[0:2, C_M2:C_M2 + 1],
                    CF[0:2, C_E2M:C_E2M + 1]),
                    waits=[("sc", sv), ("sc", sc_ne)], inc="vx")
                vv = P.do("vector", lambda v: v.tensor_copy(
                    CB[0:2, CB_NK2:CB_NK2 + 1], CF[0:2, C_NKE:C_NKE + 1]),
                    waits=fw("CBR") + [("vx", vv)], inc="vx")
                pv = P.do("tensor", lambda t: t.matmul(
                    psR[:, 1000:1001], SEL[:], CB[0:2, CB_NK2:CB_NK2 + 1],
                    start=True, stop=True),
                    waits=[("vx", vv), ("sq", sq_pro)] + fw("psR"), inc="pe")
                vv = P.do("vector", lambda v: v.tensor_copy(
                    CF[:, C_NKE:C_NKE + 1], psR[:, 1000:1001]),
                    waits=[("pe", pv)], inc="vx")
                vv = P.do("vector", lambda v: v.tensor_mul(
                    CF[:, C_NK2:C_NK2 + 1], CF[:, C_NKE:C_NKE + 1],
                    CF[:, C_NKE:C_NKE + 1]), waits=[("vx", vv)], inc="vx")
                vx_nk2 = P.do("vector", lambda v: v.tensor_copy(
                    CB[:, 1:2], CF[:, C_NK2:C_NK2 + 1]),
                    waits=[("vx", vv)], inc="vx")
                fmark("psR", "vx", vv)

                for hh in range(2):
                    h = 2 * pr + hh
                    pb = hh * 64
                    ch = h // 2
                    dl = 64 if hh == 0 else 0
                    nr0 = 0 if hh == 0 else 64
                    voff = 0 if hh == 0 else 64

                    pv = None
                    for g in range(4):
                        pv = P.do("tensor", lambda t, g=g, h=h: t.matmul(
                            psT[0:1, g * 512:(g + 1) * 512],
                            ZB[:, h:h + 1],
                            ab(0, 128, O_SIG + g * 512, [[1, 512]]),
                            start=True, stop=True),
                            waits=[("vx", vx_sig)] +
                            (fw("psT") if g == 0 else []), inc="pe")
                    sc_c2 = None
                    for g in range(4):
                        sv = P.do("scalar", lambda s, g=g: s.activation(
                            SF[0:1, 0:512], psT[0:1, g * 512:(g + 1) * 512],
                            AF.Ln, bias=1e-6),
                            waits=[("pe", pv)] +
                            (fw("SBR") if g == 0 else []) +
                            ([("sc", sc_c2)] if sc_c2 else []), inc="sc")
                        sc_c2 = P.do("scalar", lambda s, g=g: s.activation(
                            SB[0:1, g * 512:(g + 1) * 512], SF[0:1, 0:512],
                            AF.Exp, scale=-1.0,
                            bias=CF[0:1, C_LN1MG:C_LN1MG + 1]),
                            waits=[("sc", sv)], inc="sc")
                    pv = None
                    for g in range(4):
                        pv = P.do("tensor", lambda t, g=g: t.matmul(
                            psT[:, g * 512:(g + 1) * 512], ONR[0:1, 0:128],
                            SB[0:1, g * 512:(g + 1) * 512],
                            start=True, stop=True),
                            waits=[("sc", sc_c2)], inc="pe")
                    fmark("SBR", "pe", pv)
                    vx_sigc2 = None
                    for j in range(2):
                        vx_sigc2 = P.do("vector", lambda v, j=j, pb=pb:
                                        v.tensor_mul(
                            ab(pb, 64, O_SIG + j * 1024, [[1, 1024]]),
                            ab(pb, 64, O_SIG + j * 1024, [[1, 1024]]),
                            psT[pb:pb + 64, j * 1024:(j + 1) * 1024]),
                            waits=[("pe", pv)], inc="vx")
                    fmark("psT", "vx", vx_sigc2)

                    pv = None
                    for ti in (range(16) if "ublock" not in skip else []):
                        pv = P.do("tensor", lambda t, ti=ti, pb=pb: t.matmul(
                            psR[:, 1008 + ti:1009 + ti],
                            ab(pb, 64, O_XB + ti * 128, [[1, 128]]),
                            CB[pb:pb + 64, 1:2], start=True, stop=True),
                            waits=[("sc", sc_qsq), ("vx", vx_nk2)] +
                            (fw("psR") if ti == 0 else []), inc="pe")
                    fmark("CBR", "pe", pv)
                    if "ublock" in skip:
                        vx_ub = None
                        sv = None
                    else:
                        sv = P.do("scalar", lambda s: s.activation(
                            UB[:, 0:16], psR[:, 1008:1024], AF.Ln),
                            waits=[("pe", pv)] + fw("UBR"), inc="sc")
                    if "ublock" not in skip:
                        sv = P.do("scalar", lambda s: s.activation(
                            UB[:, 0:16], UB[:, 0:16], AF.Exp, scale=0.5, bias=LN8),
                            waits=[("sc", sv)], inc="sc")
                        vx_ub = P.do("vector", lambda v: v.tensor_scalar(
                            UB[:, 0:16], UB[:, 0:16], -1.0, None, ALU.mult),
                            waits=[("sc", sv)], inc="vx")
                        fclr("UBR")
                        fmark("UBR", "vx", vx_ub)
                        fmark("psR", "sc", sv)

                    pv = None
                    for cc in (range(16) if "vtm" not in skip else []):
                        pv = P.do("tensor", lambda t, cc=cc, pb=pb: t.transpose(
                            psRb[:, cc * 64:(cc + 1) * 64],
                            ab(pb, 64, O_QKV + 2 * 2048 + cc * 128, [[1, 128]]),
                            IDT[pb:pb + 64, pb:pb + 64]),
                            waits=(fw("psR") if cc == 0 else []) +
                            [("vx", vx_qkv)], inc="pe")
                    if "vtm" in skip:
                        vx_vtm = None
                    else:
                        vv = P.do("vector", lambda v, voff=voff: v.tensor_copy(
                            ab(0, 128, O_VTM + voff, [[128, 16], [1, 64]]),
                            psRb[:, 0:1024].rearrange("p (c q) -> p c q", q=64)),
                            waits=[("pe", pv)] + fw("VTM"), inc="vx")
                        fmark("psR", "vx", vv)
                        vx_vtm = vv
                    if "vtm" in dbg and s_i == 0 and h == 0:
                        qv = P.do("sync", lambda s: s.dma_start(
                            taps["vtm"][:], ab(0, 128, O_VTM, [[128, 16], [1, 128]])),
                            waits=[("vx", vx_vtm)], inc="sq", n=16)

                    NT16 = 16
                    exp_done = {}; id_done = {}; c1_done = {}; dg_done = {}
                    qz_done = {}
                    dgr_done = {}; dgev_done = {}; shear_done = {}
                    evA_done = {}; evB_done = {}; evC_done = {}
                    av_done = {}; comb_done = {}

                    def tpo(ti):
                        return O_TP if ti % 2 == 0 else O_TP2

                    def aeo(ti):
                        return O_AEXP if ti % 2 == 0 else O_AEXP2

                    def bso(ti):
                        return O_BSH if ti % 2 == 0 else O_BSH2

                    def emit_qz(ti):
                        q0 = ti * 128
                        wq = [("vx", vx_q), ("pe", id_done.get(ti - 4))]
                        if ti <= 3:
                            wq += fw("psT")
                        vq = P.do("vector", lambda v, ti=ti, q0=q0, hh=hh:
                                  v.tensor_scalar(
                            ab(0, 128, O_QZ + (ti % 4) * 128, [[1, 128]]),
                            ab(0, 128, O_QKV + q0, [[1, 128]]),
                            QM[:, hh:hh + 1], None, ALU.mult),
                            waits=wq, inc="vx")
                        qz_done[ti] = vq

                    def emit_T_p12(ti):
                        q0 = ti * 128
                        e0 = 1921 - q0
                        w1 = [("vx", qz_done[ti])]
                        if ti >= 1:
                            eb = evB_done.get(ti - 1)
                            w1 += [eb if eb else ("vx", None)]
                        else:
                            w1 += fw("psS")
                        pv1 = P.do("tensor", lambda t, ti=ti, e0=e0: t.matmul(
                            psS[:, 0:512],
                            ab(0, 128, O_QZ + (ti % 4) * 128, [[1, 128]]),
                            ab(0, 128, O_E2 + e0, [[1, 512]]),
                            start=True, stop=True), waits=w1, inc="pe")
                        pv2 = P.do("tensor", lambda t, ti=ti, e0=e0: t.matmul(
                            psS[:, 512:1024],
                            ab(0, 128, O_QZ + (ti % 4) * 128, [[1, 128]]),
                            ab(0, 128, O_E2 + e0 + 512, [[1, 512]]),
                            start=True, stop=True), inc="pe")
                        wv = [("pe", pv2)]
                        if ti >= 2:
                            wv.append(("sq", shear_done.get(ti - 2)))
                        ev = P.do("scalar", lambda s, ti=ti: s.copy(
                            ab(0, 128, tpo(ti), [[1, 1024]]), psS[:, 0:1024]),
                            waits=wv, inc="sc")
                        evA_done[ti] = ("sc", ev)

                    def emit_T_p34(ti):
                        q0 = ti * 128
                        e0 = 1921 - q0
                        pv1 = P.do("tensor", lambda t, ti=ti, e0=e0: t.matmul(
                            psS[:, 0:512],
                            ab(0, 128, O_QZ + (ti % 4) * 128, [[1, 128]]),
                            ab(0, 128, O_E2 + e0 + 1024, [[1, 512]]),
                            start=True, stop=True),
                            waits=[evA_done[ti]], inc="pe")
                        pv2 = P.do("tensor", lambda t, ti=ti, e0=e0: t.matmul(
                            psS[:, 512:1024],
                            ab(0, 128, O_QZ + (ti % 4) * 128, [[1, 128]]),
                            ab(0, 128, O_E2 + e0 + 1536, [[1, 512]]),
                            start=True, stop=True), inc="pe")
                        ev = P.do("vector", lambda v, ti=ti: v.tensor_copy(
                            ab(0, 128, tpo(ti) + 1024, [[1, 1024]]), psS[:, 0:1024]),
                            waits=[("pe", pv2)], inc="vx")
                        evB_done[ti] = ("vx", ev)

                    def emit_T_p5(ti):
                        q0 = ti * 128
                        e0 = 1921 - q0
                        w = [("sc", exp_done[ti - 1])] if ti >= 1 else fw("psT")
                        pv1 = P.do("tensor", lambda t, ti=ti, e0=e0: t.matmul(
                            psT[:, 1920:2048],
                            ab(0, 128, O_QZ + (ti % 4) * 128, [[1, 128]]),
                            ab(0, 128, O_E2 + e0 + 2048, [[1, 128]]),
                            start=True, stop=True),
                            waits=w, inc="pe")
                        ev = P.do("vector", lambda v, ti=ti: v.tensor_copy(
                            ab(0, 128, tpo(ti) + 2048, [[1, 128]]),
                            psT[:, 1920:2048]),
                            waits=[("pe", pv1)], inc="vx")
                        evC_done[ti] = ev

                    def emit_shear(ti):
                        w = [evB_done[ti], ("vx", evC_done[ti])]
                        if ti >= 2:
                            w.append(("pe", id_done.get(ti - 2)))
                        if "tp" in dbg and s_i == 0 and h == 0 and ti == 0:
                            P.do("sync", lambda s, ti=ti: s.dma_start(
                                taps["tp"][:],
                                ab(0, 128, tpo(ti), [[1, WTP]])),
                                waits=list(w), inc="sq", n=16)
                        sqv = P.do("sync", lambda s, ti=ti: s.dma_start(
                            ab(0, 128, bso(ti), [[1, 2048]]),
                            AP(AB, tpo(ti) + 127, [[ANE - 1, 128], [1, 2048]])),
                            waits=w, inc="sq", n=16)
                        shear_done[ti] = sqv
                        if "bsh" in dbg and s_i == 0 and h == 0 and ti == 0:
                            P.do("sync", lambda s, ti=ti: s.dma_start(
                                taps["bsh"][:], ab(0, 128, bso(ti), [[1, 2048]])),
                                inc="sq", n=16)

                    def emit_scores(ti):
                        q0 = ti * 128
                        for g2 in range(4):
                            w = []
                            if g2 == 0:
                                w = [("sc", exp_done.get(ti - 1))] if ti >= 1 \
                                    else fw("psT")
                            if g2 == 3:
                                w = [("vx", evC_done.get(ti))]
                            P.do("tensor",
                                lambda t, g2=g2, ti=ti: t.matmul(
                                    psT[:, g2 * 512:(g2 + 1) * 512],
                                    ab(0, 128, O_QZ + (ti % 4) * 128, [[1, 128]]),
                                    ab(0, 128, O_QKV + 2048 + g2 * 512, [[1, 512]]),
                                    start=True, stop=False), waits=w, inc="pe")

                    def emit_ids(ti):
                        lastmm = None
                        for g2 in range(4):
                            w = [("sq", shear_done[ti])] if g2 == 0 else []
                            lastmm = P.do("tensor",
                                lambda t, g2=g2, ti=ti: t.matmul(
                                    psT[:, g2 * 512:(g2 + 1) * 512], IDT[:],
                                    ab(0, 128, bso(ti) + g2 * 512, [[1, 512]]),
                                    start=False, stop=True), waits=w, inc="pe")
                        id_done[ti] = lastmm

                    def emit_exp(ti):
                        sv = P.do("scalar", lambda s, ti=ti: s.activation(
                            ab(0, 128, aeo(ti), [[1, 2048]]),
                            psT[:, 0:2048], AF.Exp,
                            bias=UB[:, ti:ti + 1], scale=0.125,
                            accum_out=CF[:, C_Z:C_Z + 1]),
                            waits=[("pe", dgr_done.get(ti - 2)),
                                   ("pe", id_done[ti]), ("vx", vx_ub)],
                            inc="sc")
                        exp_done[ti] = sv
                        if "aexp" in dbg and s_i == 0 and h == 0 and ti == 0:
                            P.do("sync", lambda s, ti=ti: s.dma_start(
                                taps["aexp"][:], ab(0, 128, aeo(ti), [[1, S]])),
                                waits=[("sc", sv)], inc="sq", n=16)
                        sv = P.do("scalar", lambda s: s.activation(
                            CF[:, C_TMP:C_TMP + 1], CF[:, C_Z:C_Z + 1], AF.Ln),
                            waits=[("sc", sv)], inc="sc")
                        sv = P.do("scalar", lambda s, ti=ti: s.activation(
                            CF[:, c1c(ti):c1c(ti) + 1], CF[:, C_TMP:C_TMP + 1],
                            AF.Exp, scale=-1.0, bias=CF[:, C_LNG:C_LNG + 1]),
                            waits=[("sc", sv),
                                   ("vx", dg_done.get(ti - 2))], inc="sc")
                        c1_done[ti] = sv

                    def emit_dg(ti):
                        vv = P.do("vector", lambda v, ti=ti: v.tensor_scalar(
                            ab(0, 128, O_GS + (ti % 2) * 128, [[1, 128]]),
                            IDT[:], CF[:, c1c(ti):c1c(ti) + 1], None, ALU.mult),
                            waits=[("sc", c1_done[ti]),
                                   ("pe", dgr_done.get(ti - 2))] +
                            (fw("GS") if ti <= 1 else []), inc="vx")
                        dg_done[ti] = vv

                    def emit_dgmm(ti, rnd):
                        pv = None
                        for cc in range(8 * rnd, 8 * rnd + 8):
                            w = []
                            if cc == 8 * rnd:
                                if rnd == 0:
                                    w = [("vx", dg_done[ti]),
                                         ("vx", dgev_done.get((ti - 1, 1)))]
                                    if ti % 4 == 0 and ti >= 4:
                                        w.append(("vx",
                                                  comb_done.get(ti // 4 - 1)))
                                    if ti == 0:
                                        w += [("vx", vx_vtm)] + fw("psR")
                                else:
                                    w = [("vx", dgev_done[(ti, 0)])]
                            pv = P.do("tensor", lambda t, cc=cc, ti=ti: t.matmul(
                                psR[:, (cc % 8) * 128:(cc % 8 + 1) * 128],
                                ab(0, 128, aeo(ti) + cc * 128, [[1, 128]]),
                                ab(0, 128, O_GS + (ti % 2) * 128, [[1, 128]]),
                                start=True, stop=True), waits=w, inc="pe")
                        if rnd == 1:
                            dgr_done[ti] = pv
                        w = [("pe", pv)]
                        if rnd == 0 and ti % 4 == 0 and ti >= 4:
                            w.append(("pe", av_done.get(ti // 4 - 1)))
                        ev = P.do("vector", lambda v, ti=ti, rnd=rnd: v.tensor_copy(
                            ab(0, 128, O_AT + (ti % 4) * 128 + rnd * 8 * 512,
                               [[512, 8], [1, 128]]),
                            psR[:, 0:1024].rearrange("p (c q) -> p c q", q=128)),
                            waits=w, inc="vx")
                        dgev_done[(ti, rnd)] = ev

                    def emit_avglob(g):
                        pv = None
                        for cc in range(16):
                            w = []
                            if cc == 0:
                                w = [("vx", dgev_done[(4 * g + 3, 1)])]
                            pv = P.do("tensor", lambda t, cc=cc: t.matmul(
                                psR[:, 0:512],
                                ab(0, 128, O_VTM + cc * 128, [[1, 128]]),
                                ab(0, 128, O_AT + cc * 512, [[1, 512]]),
                                start=(cc == 0), stop=False),
                                waits=w, inc="pe")
                        pe_gl = P.do("tensor", lambda t, g=g, h=h: t.matmul(
                            psR[:, 0:512],
                            MAU[:, h, :],
                            ab(0, 128, O_SIG + g * 512, [[1, 512]]),
                            start=False, stop=True),
                            waits=[("vx", vx_sigc2), ("gq", gq_pro)], inc="pe")
                        fmark("SIG", "pe", pe_gl)
                        av_done[g] = pe_gl
                        if "combine" in skip:
                            comb_done[g] = P.val["vx"]
                            return P.val["vx"]
                        vv = P.do("vector", lambda v, g=g, nr0=nr0, ch=ch:
                                  v.tensor_copy(
                            combv(nr0, 64, ch, g * 512, 512),
                            psR[nr0:nr0 + 64, 0:512]),
                            waits=[("pe", pe_gl)] + fw("COMB"), inc="vx")
                        comb_done[g] = vv
                        return vv

                    if "tshear" not in skip:
                        emit_qz(0)
                        emit_T_p12(0)
                        emit_T_p34(0)
                        emit_T_p5(0)
                        emit_shear(0)
                        emit_qz(1)
                        emit_T_p12(1)

                    for ti in range(NT16 + 1):
                        if ti >= 1 and "transp" not in skip:
                            emit_dgmm(ti - 1, 0)
                        if ti <= NT16 - 1 and "scores" not in skip:
                            emit_scores(ti)
                        if ti >= 1 and "transp" not in skip:
                            emit_dgmm(ti - 1, 1)
                        if ti <= NT16 - 1 and "scores" not in skip:
                            emit_ids(ti)
                        if ti + 1 <= NT16 - 1 and "tshear" not in skip:
                            emit_T_p34(ti + 1)
                        if ti >= 4 and ti % 4 == 0 and "avglob" not in skip:
                            vx_comb = emit_avglob(ti // 4 - 1)
                        if ti + 2 <= NT16 - 1 and "tshear" not in skip:
                            emit_qz(ti + 2)
                        if ti <= NT16 - 1 and "scores" not in skip \
                                and "expscale" not in skip:
                            emit_exp(ti)
                            emit_dg(ti)
                        if ti + 1 <= NT16 - 1 and "tshear" not in skip:
                            emit_T_p5(ti + 1)
                            emit_shear(ti + 1)
                        if ti + 2 <= NT16 - 1 and "tshear" not in skip:
                            emit_T_p12(ti + 2)
                    if "avglob" in skip:
                        vx_comb = P.val["vx"]
                    fmark("psR", "vx", vx_comb)
                    if "comb" in dbg and s_i == 0 and h == NH - 1:
                        P.do("sync", lambda s: s.dma_start(
                            taps["comb"][:], ab(0, 128, O_COMB,
                                                [[2048, DC], [1, S]])),
                            waits=[("vx", vx_comb)], inc="sq", n=16)
                    fmark("psT", "sc", exp_done.get(15))
                    fmark("psT", "pe", id_done.get(15))
                    fmark("psS", "vx", evB_done.get(15, (None, None))[1])
                    fmark("psT", "vx", evC_done.get(15))

                fclr("QKV")
                fmark("QKV", "pe", P.val["pe"])
                fmark("QKV", "vx", P.val["vx"])
                fmark("QKV", "sc", P.val["sc"])
                fclr("SIG")
                fmark("SIG", "pe", P.val["pe"])
                fmark("SIG", "vx", P.val["vx"])
                fmark("WO", "pe", P.val["pe"])
                fmark("WO", "vx", P.val["vx"])
                fclr("XB")
                fmark("XB", "pe", P.val["pe"])
            fclr("COMB")
            fmark("COMB", "vx", P.val["vx"])
            if "comb" in dbg and s_i == 0:
                qv = P.do("sync", lambda s: s.dma_start(
                    taps["comb"][:], ab(0, 128, O_COMB, [[2048, DC], [1, S]])),
                    waits=fw("COMB"), inc="sq", n=16)

            if stage == "pairs":
                break
            vx_combv = P.val["vx"]
            wo_dma = {}
            wo_mm = {}
            bank_ev = {}

            def emit_wo_dma(oc):
                wob = O_WO if oc % 2 == 0 else O_WP
                gv = P.do("gpsimd", lambda g, oc=oc, wob=wob: g.dma_start(
                    ab(0, 128, wob, [[128, 8], [1, 128]]),
                    w_d["woT"][:, oc * 128:(oc + 1) * 128].rearrange(
                        "(c p) o -> p c o", p=128)),
                    waits=([("pe", wo_mm.get(oc - 2))] if oc >= 2 else
                           fw("WO", "WP")), inc="gq", n=16)
                wo_dma[oc] = gv

            emit_wo_dma(0)
            emit_wo_dma(1)
            for oc in range(DC):
                wob = O_WO if oc % 2 == 0 else O_WP
                lastmm = None
                for cg in range(4):
                    for dcc in range(DC):
                        wts = []
                        if dcc == 0:
                            wts = [("gq", wo_dma[oc]), ("vx", vx_combv),
                                   ("vx", bank_ev.get((oc - 1, cg)))]
                            if oc == 0 and cg == 0:
                                wts += fw("psT")
                        lastmm = P.do("tensor",
                                      lambda t, cg=cg, dcc=dcc, wob=wob: t.matmul(
                            psT[:, cg * 512:(cg + 1) * 512],
                            ab(0, 128, wob + dcc * 128, [[1, 128]]),
                            ab(0, 128, O_COMB + dcc * 2048 + cg * 512, [[1, 512]]),
                            start=(dcc == 0), stop=(dcc == DC - 1)),
                            waits=wts, inc="pe")
                    vv = P.do("vector", lambda v, oc=oc, cg=cg: v.tensor_scalar(
                        ab(0, 128, O_H1 + oc * 2048 + cg * 512, [[1, 512]]),
                        psT[:, cg * 512:(cg + 1) * 512],
                        BIA[:, 24 + oc:24 + oc + 1], None, ALU.add),
                        waits=[("pe", lastmm)] +
                        (fw("H1") if oc == 0 and cg == 0 else []), inc="vx")
                    bank_ev[(oc, cg)] = vv
                wo_mm[oc] = lastmm
                fmark("COMB", "pe", lastmm)
                if oc + 2 <= DC - 1:
                    emit_wo_dma(oc + 2)
            fclr("WO")
            fmark("WO", "pe", lastmm)
            fmark("WP", "pe", lastmm)
            fmark("psT", "vx", vv)
            vx_y1 = vv
            fclr("H1")
            fmark("H1", "vx", vx_y1)
            if "y1" in dbg and s_i == 0:
                qv = P.do("sync", lambda s: s.dma_start(
                    taps["y1"][:], ab(0, 128, O_H1, [[2048, DC], [1, S]])),
                    waits=[("vx", vx_y1)], inc="sq", n=16)

            sq_xscr_half = {}
            for half in range(2):
                hc0 = half * 1024
                pe_st = emit_ln_stats(
                    lambda c: ab(0, 128, O_H1 + c * 2048 + hc0, [[1, 1024]]),
                    lambda c: [("vx", vx_y1)], True,
                    lambda g2: psT[0:1, g2 * 512:(g2 + 1) * 512],
                    lambda g2: psT[0:1, 1024 + g2 * 512:1024 + (g2 + 1) * 512],
                    src_rg=lambda c: "H1")
                vxr = emit_mrstd(pe_st, psT[0:1, 0:1024], psT[0:1, 1024:2048], 1024)
                fmark("psT", "vx", vxr)
                fmark("XROW", "vx", vxr)
                pvb = []
                for g2 in range(2):
                    pv = emit_bcast(vxr, g2, 0,
                                    psT[:, g2 * 1024:g2 * 1024 + 512],
                                    psT[:, g2 * 1024 + 512:g2 * 1024 + 1024])
                    pvb.append(pv)
                fmark("SBR", "pe", pvb[-1])
                lastsq = None
                for c in range(DC):
                    buf = X_A if c % 2 == 0 else X_B
                    rg = "XA" if c % 2 == 0 else "XBUF"
                    sqv = P.do("sync", lambda s, c=c, buf=buf, s_i=s_i, hc0=hc0:
                               s.dma_start(
                        xf(0, 128, buf, [[1, 1024]]),
                        x_d[s_i, c * 128:(c + 1) * 128, hc0:hc0 + 1024]),
                        waits=fw(rg), inc="sq", n=16)
                    for g2 in range(2):
                        vv = P.do("vector", lambda v, c=c, g2=g2, hc0=hc0: v.tensor_sub(
                            SF[:, 0:512],
                            ab(0, 128, O_H1 + c * 2048 + hc0 + g2 * 512, [[1, 512]]),
                            psT[:, g2 * 1024:g2 * 1024 + 512]),
                            waits=[("pe", pvb[g2])], inc="vx")
                        vv = P.do("vector", lambda v, g2=g2: v.tensor_mul(
                            SF[:, 0:512], SF[:, 0:512],
                            psT[:, g2 * 1024 + 512:g2 * 1024 + 1024]),
                            waits=[("vx", vv)], inc="vx")
                        vv = P.do("vector", lambda v, buf=buf, g2=g2: v.tensor_add(
                            xf(0, 128, buf + g2 * 512, [[1, 512]]),
                            xf(0, 128, buf + g2 * 512, [[1, 512]]),
                            SF[:, 0:512]), waits=[("sq", sqv), ("vx", vv)], inc="vx")
                    sqw = P.do("sync", lambda s, c=c, buf=buf, s_i=s_i, hc0=hc0: s.dma_start(
                        xs_d[s_i, c * 128:(c + 1) * 128, hc0:hc0 + 1024],
                        xf(0, 128, buf, [[1, 1024]])),
                        waits=[("vx", vv)], inc="sq", n=16)
                    fmark(rg, "sq", sqw)
                    lastsq = sqw
                fmark("psT", "vx", vv)
                sq_xscr_half[half] = lastsq
            sq_xscr = lastsq
            phase_mark()

            if stage == "postln":
                break
            for tt in range(NTT):
                tc0 = tt * 512
                sqx_c = {}
                for c in range(DC):
                    sqx_c[c] = P.do("sync",
                        lambda s, tc0=tc0, s_i=s_i, c=c: s.dma_start(
                        xf(0, 128, X_NT + c * 512, [[1, 512]]),
                        xs_d[s_i, c * 128:(c + 1) * 128, tc0:tc0 + 512]),
                        waits=[("sq", sq_xscr_half[tt // 2])] +
                        (fw("XNT") if c == 0 else []), inc="sq", n=16)
                pe_st = emit_ln_stats(
                    lambda c: xf(0, 128, X_NT + c * 512, [[1, 512]]),
                    lambda c: [("sq", sqx_c[c])], False,
                    lambda g2: psT[0:1, 0:512],
                    lambda g2: psT[0:1, 512:1024], width=512,
                    src_rg=lambda c: "XNT")
                vxr = emit_mrstd(pe_st, psT[0:1, 0:512], psT[0:1, 512:1024], 512)
                fmark("psT", "vx", vxr)
                fmark("XROW", "vx", vxr)
                pv = emit_bcast(vxr, 0, 0, psT[:, 1024:1536], psT[:, 1536:2048])
                fmark("SBR", "pe", pv)
                pvb2 = pv
                lastvx = None
                for c in range(DC):
                    vv = P.do("vector", lambda v, c=c: v.tensor_sub(
                        SF[:, 0:512], xf(0, 128, X_NT + c * 512, [[1, 512]]),
                        psT[:, 1024:1536]),
                        waits=[("pe", pvb2)] +
                        (fw("H2", "AEXP", "TP", "BSH") if c == 0 else []),
                        inc="vx")
                    vv = P.do("vector", lambda v, c=c: v.tensor_mul(
                        ab(0, 128, O_FH2 + c * 512, [[1, 512]]),
                        SF[:, 0:512], psT[:, 1536:2048]),
                        waits=[("vx", vv)], inc="vx")
                    lastvx = vv
                vx_h2 = lastvx
                fclr("H2")
                if "h2" in dbg and s_i == 0 and tt == 0:
                    qv = P.do("sync", lambda s: s.dma_start(
                        taps["h2"][:], ab(0, 128, O_FH2, [[512, DC], [1, 512]])),
                        waits=[("vx", vx_h2)], inc="sq", n=16)
                def gu_off(sci):
                    return (O_QKV, O_QKV + 3072) if sci % 2 == 0 else \
                        (50176, 53248)
                gq_gu = {}
                mm_gu = {}

                def emit_gu_dma(sci):
                    og, ou = gu_off(sci)
                    w = fw("FFNW", "QKV") if sci < 2 else \
                        [("pe", mm_gu.get(sci - 2))]
                    gvg = P.do("gpsimd", lambda g, sci=sci, og=og: g.dma_start(
                        ab(0, 128, og, [[384, 8], [1, 384]]),
                        wg_d[:, sci * 384:(sci + 1) * 384].rearrange(
                            "(c p) h -> p c h", p=128)),
                        waits=w, inc="gq", n=16)
                    gvu = P.do("gpsimd", lambda g, sci=sci, ou=ou: g.dma_start(
                        ab(0, 128, ou, [[384, 8], [1, 384]]),
                        wu_d[:, sci * 384:(sci + 1) * 384].rearrange(
                            "(c p) h -> p c h", p=128)), inc="gq", n=16)
                    gq_gu[sci] = gvu

                emit_gu_dma(0)
                emit_gu_dma(1)
                for sci in range(FSUP):
                    og, ou = gu_off(sci)
                    for j in range(3):
                        hc = sci * 3 + j
                        lastg = None
                        for dcc in range(DC):
                            lastg = P.do("tensor", lambda t, j=j, dcc=dcc, og=og:
                                         t.matmul(
                                psS[:, 0:512],
                                ab(0, 128, og + dcc * 384 + j * 128, [[1, 128]]),
                                ab(0, 128, O_FH2 + dcc * 512, [[1, 512]]),
                                start=(dcc == 0), stop=(dcc == DC - 1)),
                                waits=[("gq", gq_gu[sci]), ("vx", vx_h2)] +
                                (fw("psS") if dcc == 0 else []), inc="pe")
                        sv = P.do("scalar", lambda s: s.activation(
                            ab(0, 128, O_GS, [[1, 512]]), psS[:, 0:512],
                            AF.Silu, scale=DQ),
                            waits=[("pe", lastg)] + fw("GS"), inc="sc")
                        lastu = None
                        for dcc in range(DC):
                            lastu = P.do("tensor", lambda t, j=j, dcc=dcc, ou=ou:
                                         t.matmul(
                                psS[:, 512:1024],
                                ab(0, 128, ou + dcc * 384 + j * 128,
                                   [[1, 128]]),
                                ab(0, 128, O_FH2 + dcc * 512, [[1, 512]]),
                                start=(dcc == 0), stop=(dcc == DC - 1)),
                                inc="pe")
                        vv = P.do("vector", lambda v, hc=hc: v.scalar_tensor_tensor(
                            ab(0, 128, O_COMB + hc * 512, [[1, 512]]),
                            psS[:, 512:1024], DQ,
                            ab(0, 128, O_GS, [[1, 512]]), ALU.mult, ALU.mult),
                            waits=[("pe", lastu), ("sc", sv)] + fw("COMB"),
                            inc="vx")
                        fmark("GS", "vx", vv)
                        fmark("psS", "vx", vv)
                    mm_gu[sci] = lastu
                    fmark("FFNW", "pe", lastu)
                    if sci + 2 < FSUP:
                        emit_gu_dma(sci + 2)
                fmark("H2", "pe", lastu)
                vx_h3 = vv
                if "h3" in dbg and s_i == 0 and tt == 0:
                    qv = P.do("sync", lambda s: s.dma_start(
                        taps["h3"][:], ab(0, 128, O_COMB, [[512, FHC], [1, 512]])),
                        waits=[("vx", vx_h3)], inc="sq", n=16)
                def dreg(dcc):
                    if dcc < 2:
                        return psS[:, dcc * 512:(dcc + 1) * 512]
                    if dcc < 6:
                        return psT[:, (dcc - 2) * 512:(dcc - 1) * 512]
                    return psR[:, (dcc - 6) * 512:(dcc - 5) * 512]
                dwaits = fw("psS", "psT", "psR")
                def wd_off(sci):
                    return O_SIG if sci % 2 == 0 else O_WP
                gq_wd = {}
                mm_wd = {}

                def emit_wd_dma(sci):
                    od = wd_off(sci)
                    w = fw("FFNW", "SIG") if sci < 2 else \
                        [("pe", mm_wd.get(sci - 2))]
                    gvd = P.do("gpsimd", lambda g, sci=sci, od=od: g.dma_start(
                        ab(0, 128, od, [[1024, 3], [1, 1024]]),
                        wd_d[sci * 384:(sci + 1) * 384, :].rearrange(
                            "(j p) d -> p j d", p=128)),
                        waits=w, inc="gq", n=16)
                    gq_wd[sci] = gvd

                emit_wd_dma(0)
                emit_wd_dma(1)
                for sci in range(FSUP):
                    od = wd_off(sci)
                    for j in range(3):
                        hc = sci * 3 + j
                        for dcc in range(DC):
                            pv = P.do("tensor", lambda t, j=j, dcc=dcc, hc=hc, od=od:
                                      t.matmul(
                                dreg(dcc),
                                ab(0, 128, od + j * 1024 + dcc * 128, [[1, 128]]),
                                ab(0, 128, O_COMB + hc * 512, [[1, 512]]),
                                start=(hc == 0), stop=(hc == FHC - 1)),
                                waits=[("gq", gq_wd[sci]), ("vx", vx_h3)] +
                                (dwaits if hc == 0 else []), inc="pe")
                    mm_wd[sci] = pv
                    fmark("FFNW", "pe", pv)
                    if sci + 2 < FSUP:
                        emit_wd_dma(sci + 2)
                pe_dn = pv
                fmark("COMB", "pe", pe_dn)
                for dcc in range(DC):
                    vv = P.do("vector", lambda v, dcc=dcc: v.tensor_scalar(
                        ab(0, 128, O_FY2 + dcc * 512, [[1, 512]]),
                        dreg(dcc), DQ, None, ALU.mult),
                        waits=[("pe", pe_dn)] +
                        (fw("Y2", "AT", "BSH") if dcc == 0 else []),
                        inc="vx")
                vx_y2 = vv
                fmark("psS", "vx", vx_y2)
                fmark("psT", "vx", vx_y2)
                fmark("psR", "vx", vx_y2)
                fclr("Y2")
                if "y2" in dbg and s_i == 0 and tt == 0:
                    qv = P.do("sync", lambda s: s.dma_start(
                        taps["y2"][:], ab(0, 128, O_FY2, [[512, DC], [1, 512]])),
                        waits=[("vx", vx_y2)], inc="sq", n=16)
                pe_st = emit_ln_stats(
                    lambda c: ab(0, 128, O_FY2 + c * 512, [[1, 512]]),
                    lambda c: [("vx", vx_y2)], True,
                    lambda g2: psT[0:1, 0:512],
                    lambda g2: psT[0:1, 512:1024], width=512,
                    src_rg=lambda c: "Y2")
                vxr = emit_mrstd(pe_st, psT[0:1, 0:512], psT[0:1, 512:1024], 512)
                fmark("psT", "vx", vxr)
                fmark("XROW", "vx", vxr)
                pv = emit_bcast(vxr, 0, 0, psT[:, 1024:1536], psT[:, 1536:2048])
                fmark("SBR", "pe", pv)
                for c in range(DC):
                    vv = P.do("vector", lambda v, c=c: v.tensor_sub(
                        SF[:, 0:512],
                        ab(0, 128, O_FY2 + c * 512, [[1, 512]]),
                        psT[:, 1024:1536]),
                        waits=[("pe", pv)] + (fw("OUT") if c == 0 else []),
                        inc="vx")
                    vv = P.do("vector", lambda v: v.tensor_mul(
                        SF[:, 0:512], SF[:, 0:512], psT[:, 1536:2048]),
                        waits=[("vx", vv)], inc="vx")
                    vv = P.do("vector", lambda v, c=c: v.tensor_add(
                        xf(0, 128, X_A + c * 512, [[1, 512]]),
                        xf(0, 128, X_NT + c * 512, [[1, 512]]),
                        SF[:, 0:512]), waits=[("vx", vv)], inc="vx")
                fmark("psT", "vx", vv)
                fmark("Y2", "vx", vv)
                fclr("OUT")
                sqo = P.do("sync", lambda s, tc0=tc0, s_i=s_i: s.dma_start(
                    y_d[s_i, :, tc0:tc0 + 512].rearrange("(c p) t -> p c t", p=128),
                    xf(0, 128, X_A, [[512, 8], [1, 512]])),
                    waits=[("vx", vv)], inc="sq", n=16)
                fmark("OUT", "sq", sqo)
                fmark("XA", "sq", sqo)
                fmark("XBUF", "sq", sqo)
                sq_out_total[0] = sqo
                fmark("XNT", "vx", vv)
            phase_mark()

        P.do("gpsimd", lambda g: None, waits=[("sq", P.val["sq"])])
        P.do("sync", lambda s: None, waits=[("gq", P.val["gq"])])

        P.check()

        with nc.Block() as block:
            def mk(eng_name):
                def emit(eng):
                    for ws, fn, inc, n in P.ops[eng_name]:
                        for s, v in ws:
                            eng.wait_ge(sems[s], v)
                        inst = fn(eng)
                        if inc is not None and inst is not None:
                            inst.then_inc(sems[inc], n)
                return emit
            block.tensor(mk("tensor"))
            block.vector(mk("vector"))
            block.scalar(mk("scalar"))
            block.sync(mk("sync"))
            block.gpsimd(mk("gpsimd"))

    return nc, taps


_HOST = {}


def _host_prep(inputs):
    x = np.asarray(inputs["L0_attn_hidden_states"])
    d = {}
    d["wqT"] = np.ascontiguousarray(np.asarray(inputs["L0_attn_Wq"]).T)
    d["wkT"] = np.ascontiguousarray(np.asarray(inputs["L0_attn_Wk"]).T)
    d["wvT"] = np.ascontiguousarray(np.asarray(inputs["L0_attn_Wv"]).T)
    perm = np.empty(D, np.int64)
    for c in range(DC):
        for p in range(128):
            h = 2 * c + (1 if p >= 64 else 0)
            perm[c * 128 + p] = 64 * h + (p % 64)
    d["woT"] = np.ascontiguousarray(np.asarray(inputs["L0_attn_Wo"]).T[:, :])
    d["woT"] = np.ascontiguousarray(d["woT"][perm, :])
    for nm, key in (("bq", "L0_attn_bq"), ("bk", "L0_attn_bk"),
                    ("bv", "L0_attn_bv"), ("bo", "L0_attn_bo")):
        d[nm] = np.ascontiguousarray(
            np.asarray(inputs[key]).reshape(8, 128).T.astype(np.float32))
    rel = np.asarray(inputs["L0_attn_rel_emb"])
    m_idx = 2048 - np.clip(np.arange(EPW) - 1024, 0, 2048)
    e2h = np.ascontiguousarray(rel[m_idx].T.astype(np.float32))
    d["e2d"] = np.ascontiguousarray(np.concatenate([e2h, e2h], axis=0))
    M = np.asarray(inputs["L0_attn_M"]); z = np.asarray(inputs["L0_attn_z"])
    ma = np.zeros((128, NH, 128), np.float32)
    for h in range(NH):
        if h % 2 == 0:
            ma[0:64, h, 0:64] = M[h]; ma[0:64, h, 64] = z[h]
        else:
            ma[64:128, h, 64:128] = M[h]; ma[64:128, h, 0] = z[h]
    d["maugd"] = ma
    zb = np.zeros((128, NH), np.float32)
    for h in range(NH):
        p0 = (h % 2) * 64
        zb[p0:p0 + 64, h] = z[h]
    d["zbd"] = zb.astype(ml_dtypes.bfloat16)
    d["wgT"] = np.ascontiguousarray(np.asarray(inputs["L0_ffn_w_gate"]).T)
    d["wuT"] = np.ascontiguousarray(np.asarray(inputs["L0_ffn_w_up"]).T)
    d["wdT"] = np.ascontiguousarray(np.asarray(inputs["L0_ffn_w_down"]).T)
    d["galpha"] = np.asarray(inputs["L0_attn_gate_alpha"],
                             np.float32).reshape(1, 1)
    d["idd"] = np.eye(128, dtype=ml_dtypes.bfloat16)
    sel = np.zeros((2, 128), np.float32)
    sel[0, 0:64] = 1.0; sel[1, 64:128] = 1.0
    d["seld"] = sel.astype(ml_dtypes.bfloat16)
    xs = []
    for c in range(8):
        xc = np.ascontiguousarray(
            x[2 * c:2 * c + 2].transpose(0, 2, 1)).astype(np.float32)
        xs.append(xc)
    return d, xs


def kernel(**inputs):
    shared, xs = _host_prep(inputs)
    nc, _taps = build_program()
    in_maps = []
    for c in range(8):
        m = {"x": xs[c]}
        m.update(shared)
        in_maps.append(m)
    res = run_bass_kernel_spmd(nc, in_maps, core_ids=list(range(8)))
    out = np.empty((16, S, D), np.float32)
    for c in range(8):
        y = res.results[c]["yout"]
        out[2 * c] = y[0].T
        out[2 * c + 1] = y[1].T
    return out
